# version 07 (final)
# baseline (speedup 1.0000x reference)
import os
os.environ.setdefault("TRN_ALLOW_ICI_ALL_TO_ALL", "1")
import numpy as np
import ml_dtypes
import concourse.bass as bass
import concourse.mybir as mybir
from concourse.bass_utils import run_bass_kernel_spmd

RG8 = [[0, 1, 2, 3, 4, 5, 6, 7]]

F32 = mybir.dt.float32
BF16 = mybir.dt.bfloat16
AF = mybir.ActivationFunctionType
ALU = mybir.AluOpType
AXX = mybir.AxisListType.X
BF = ml_dtypes.bfloat16

NCORES = 8
B = 16
S = 2048
D = 1024
SL = S // NCORES
NTOK = B * SL
NT = NTOK // 128
BL = B // NCORES
MAGIC = 12582912.0
LN_EPS = 1e-5
RMS_EPS = float(np.finfo(np.float32).eps)
NEGBIG = -1e9
QB = 127.0
BIT_EPS = 1e-5

AH, AKV, HD = 16, 4, 64
LW, GT = 256, 64
ROPE_BASE = 10000.0

GH, GDK, GLR = 8, 128, 16
QPERM = [0, 4, 1, 5, 2, 6, 3, 7, 8, 12, 9, 13, 10, 14, 11, 15]
QPOS = {h: i for i, h in enumerate(QPERM)}
GNORM = 16.0
GLA_EPS = 1e-5
CHUNK = 128
NCH = S // CHUNK


class Sched:
    ENGS = ("sync", "gpsimd", "vector", "scalar", "tensor")

    def __init__(self, nc):
        self.nc = nc
        self.q = {e: [] for e in self.ENGS}
        self.sem = {}
        self.cnt = {}
        self.last_wait = {e: {} for e in self.ENGS}
        for name in ("dX", "dW", "PE", "ACT", "DVE", "GP", "CC"):
            self.sem[name] = nc.alloc_semaphore(name)
            self.cnt[name] = 0

    def _filter(self, eng, waits):
        lw = self.last_wait[eng]
        eff = []
        for sname, val in waits:
            if val <= 0 or lw.get(sname, 0) >= val:
                continue
            lw[sname] = val
            eff.append((self.sem[sname], val))
        return eff

    OWN = {"vector": "DVE", "scalar": "ACT", "tensor": "PE", "gpsimd": "GP"}

    def emit(self, eng, fn, waits=(), sem=None, inc=1, selfwait=False):
        if selfwait:
            own = self.OWN[eng]
            waits = list(waits) + [(own, self.cnt[own])]
        eff = self._filter(eng, waits)
        ret = None
        semh = None
        if sem is not None:
            self.cnt[sem] += inc
            ret = self.cnt[sem]
            semh = self.sem[sem]

        def closure(e, fn=fn, eff=tuple(eff), semh=semh, inc=inc):
            for sh, val in eff:
                e.wait_ge(sh, val)
            ins = fn(e)
            if semh is not None:
                ins.then_inc(semh, inc)

        self.q[eng].append(closure)
        return ret

    def wait(self, eng, waits):
        eff = self._filter(eng, waits)
        if not eff:
            return

        def closure(e, eff=tuple(eff)):
            for sh, val in eff:
                e.wait_ge(sh, val)

        self.q[eng].append(closure)

    def val(self, sem):
        return self.cnt[sem]

    def snap(self):
        return dict(self.cnt)

    def fence(self):
        snap = [(s, self.cnt[s]) for s in self.sem]
        for eng in self.ENGS:
            self.wait(eng, snap)

    def replay(self):
        with self.nc.Block() as blk:
            for eng in self.ENGS:
                ops = self.q[eng]

                def run(e, ops=ops):
                    for op in ops:
                        op(e)

                getattr(blk, eng)(run)


class Pools:
    def __init__(self, nc):
        self.wb = nc.alloc_sbuf_tensor("arena_bf", [128, 49152], BF16)
        self.lnx = nc.alloc_sbuf_tensor("lnx_fm", [128, 32768], BF16)
        self.f32 = nc.alloc_sbuf_tensor("arena_f32", [128, 9216], F32)
        self.stat = nc.alloc_sbuf_tensor("stats", [128, 256], F32)
        self.sm = nc.alloc_sbuf_tensor("small_bf", [128, 4096], BF16)
        self.smf = nc.alloc_sbuf_tensor("small_f32", [128, 512], F32)
        self.psA = nc.alloc_psum_tensor("psA", [128, 2048], F32)
        self.psB = nc.alloc_psum_tensor("psB", [128, 1024], F32)
        self.pT = [nc.alloc_psum_tensor(f"pT{i}", [128, 1024], BF16) for i in range(2)]
        self.ident = self.sm[:, 0:128]
        self.cst = nc.alloc_sbuf_tensor("cconst", [128, 16], F32)
        self.cmap = {}

    def const_col(self, S_, val):
        if val not in self.cmap:
            j = len(self.cmap)
            assert j < 16
            ap = self.cst[:, j:j + 1]
            S_.emit("vector", (lambda e, ap=ap, val=val: e.memset(ap, val)),
                    sem="DVE")
            self.cmap[val] = (ap, S_.val("DVE"))
        return self.cmap[val]


def emit_ln_transpose(S_, P, x_dram, ntiles=NT):
    XT = [P.f32[:, i * 1024:(i + 1) * 1024] for i in range(3)]
    SQ = P.f32[:, 3 * 1024:4 * 1024]
    LNT = P.sm[:, 1024:2048]
    rstd = P.stat[:, 0:32]
    mean = P.stat[:, 32:64]
    sums = P.stat[:, 64:96]
    sumsq = P.stat[:, 96:128]
    var = P.stat[:, 128:160]
    negmr = P.stat[:, 192:224]
    lnx3 = P.lnx.ap().rearrange("p (kc tok) -> p kc tok", kc=8)

    dve_done, act_done, pe_done = [], [], []
    for t in range(ntiles):
        xt = XT[t % 3]
        w = []
        if t >= 3:
            w = [("DVE", dve_done[t - 3]), ("ACT", act_done[t - 3])]
        vload = S_.emit("sync", (lambda e, xt=xt, t=t: e.dma_start(
            out=xt, in_=x_dram[t * 128:(t + 1) * 128, :])), waits=w, sem="dX", inc=16)
        S_.emit("vector", (lambda e, xt=xt, t=t: e.reduce_sum(
            sums[:, t:t + 1], xt, axis=AXX)), waits=[("dX", vload)], sem="DVE")
        S_.emit("scalar", (lambda e, xt=xt, t=t: e.activation(
            out=SQ, in_=xt, func=AF.Square, accum_out=sumsq[:, t:t + 1])),
            waits=[("dX", vload)], sem="ACT")
        sq_v = S_.val("ACT")
        S_.emit("vector", (lambda e, t=t: e.tensor_scalar(
            out=mean[:, t:t + 1], in0=sums[:, t:t + 1], scalar1=1.0 / D,
            scalar2=None, op0=ALU.mult)), sem="DVE", selfwait=True)
        S_.emit("vector", (lambda e, t=t: e.scalar_tensor_tensor(
            out=var[:, t:t + 1], in0=sums[:, t:t + 1], scalar=mean[:, t:t + 1],
            in1=sumsq[:, t:t + 1], op0=ALU.mult, op1=ALU.subtract)),
            waits=[("ACT", sq_v)], sem="DVE", selfwait=True)
        vfin = S_.val("DVE")
        eps_ap, eps_v = P.const_col(S_, LN_EPS)
        nid_ap, nid_v = P.const_col(S_, -1.0 / D)
        S_.emit("scalar", (lambda e, t=t: e.activation(
            out=P.stat[:, 160 + t:161 + t], in_=var[:, t:t + 1], func=AF.Sqrt,
            scale=nid_ap, bias=eps_ap)),
            waits=[("DVE", max(vfin, eps_v, nid_v))], sem="ACT")
        sstd = S_.val("ACT")
        S_.emit("vector", (lambda e, t=t: e.reciprocal(
            rstd[:, t:t + 1], P.stat[:, 160 + t:161 + t])),
            waits=[("ACT", sstd)], sem="DVE")
        S_.emit("vector", (lambda e, t=t: e.scalar_tensor_tensor(
            out=negmr[:, t:t + 1], in0=mean[:, t:t + 1], scalar=-1.0,
            in1=rstd[:, t:t + 1], op0=ALU.mult, op1=ALU.mult)),
            sem="DVE", selfwait=True)
        stats_v = S_.val("DVE")
        aw = [("DVE", stats_v)]
        if t >= 1:
            aw.append(("PE", pe_done[t - 1]))
        S_.emit("scalar", (lambda e, xt=xt, t=t: e.activation(
            out=LNT, in_=xt, func=AF.Identity, scale=rstd[:, t:t + 1],
            bias=negmr[:, t:t + 1])), waits=aw, sem="ACT")
        act_done.append(S_.val("ACT"))
        apl = S_.val("ACT")
        pT = P.pT[t % 2]
        tw = [("ACT", apl)]
        if t >= 2:
            tw.append(("DVE", dve_done[t - 2]))
        for kc in range(8):
            S_.emit("tensor", (lambda e, kc=kc, pT=pT: e.transpose(
                pT[:, kc * 128:(kc + 1) * 128], LNT[:, kc * 128:(kc + 1) * 128],
                P.ident)), waits=tw if kc == 0 else (), sem="PE")
        pe_done.append(S_.val("PE"))
        tv = S_.val("PE")
        S_.emit("vector", (lambda e, pT=pT, t=t: e.tensor_copy(
            lnx3[:, :, t * 128:(t + 1) * 128],
            pT[:].rearrange("p (kc c) -> p kc c", kc=8))),
            waits=[("PE", tv)], sem="DVE")
        dve_done.append(S_.val("DVE"))
    return {"DVE": S_.val("DVE"), "ACT": S_.val("ACT"), "PE": S_.val("PE")}


def emit_ffn0(S_, P, d, x_in, x_out):
    nc = S_.nc
    Wx = P.wb[:, 0:24576]
    Ux = P.wb[:, 24576:49152]
    for kc in range(8):
        S_.emit("gpsimd", (lambda e, kc=kc: e.dma_start(
            out=Wx[:, kc * 3072:(kc + 1) * 3072],
            in_=d["ffn0_wx_t"][kc * 128:(kc + 1) * 128, :])), sem="dW", inc=16)
        S_.emit("gpsimd", (lambda e, kc=kc: e.dma_start(
            out=Ux[:, kc * 3072:(kc + 1) * 3072],
            in_=d["ffn0_ux_t"][kc * 128:(kc + 1) * 128, :])), sem="dW", inc=16)
    w_all = S_.val("dW")

    inc_f = P.smf[:, 0:8]
    b_f = P.smf[:, 8:32]
    bias0 = P.smf[:, 32:56]
    inc_b = P.sm[:, 128:136]
    onesr = P.sm[0:1, 384:512]
    c_b = P.sm[:, 2048:4096]
    S_.emit("gpsimd", (lambda e: e.dma_start(
        out=inc_f, in_=d["ffn0_in_c"].rearrange("(c p) -> p c", p=128),
        allow_slow_non_contiguous=True)), sem="dW", inc=16)
    S_.emit("gpsimd", (lambda e: e.dma_start(
        out=b_f, in_=d["ffn0_b_cat"].rearrange("(c p) -> p c", p=128),
        allow_slow_non_contiguous=True)), sem="dW", inc=16)
    w_sm = S_.val("dW")
    S_.emit("vector", (lambda e: e.tensor_copy(inc_b, inc_f)),
            waits=[("dW", w_sm)], sem="DVE")
    S_.emit("vector", (lambda e: e.memset(onesr, 1.0)), sem="DVE")
    incb_v = S_.val("DVE")

    for oc in range(24):
        for kc in range(8):
            S_.emit("tensor", (lambda e, oc=oc, kc=kc: e.matmul(
                P.psB[:, oc:oc + 1],
                Ux[:, kc * 3072 + oc * 128: kc * 3072 + (oc + 1) * 128],
                inc_b[:, kc:kc + 1],
                start=(oc == 0 and kc == 0), stop=(oc == 23 and kc == 7))),
                waits=[("dW", w_all), ("DVE", incb_v)] if oc == 0 and kc == 0 else (),
                sem="PE")
    pe_b0 = S_.val("PE")
    S_.emit("vector", (lambda e: e.tensor_add(bias0, b_f, P.psB[:, 0:24])),
            waits=[("PE", pe_b0), ("dW", w_sm)], sem="DVE")

    c_f = P.f32[:, 4096:6144]
    S_.emit("vector", (lambda e: e.tensor_copy(
        c_f.rearrange("p (kc s) -> p kc s", kc=8),
        inc_f.unsqueeze(2).broadcast_to([128, 8, 256]))),
        waits=[("dW", w_sm)], sem="DVE")
    S_.emit("vector", (lambda e: e.tensor_copy(c_b, c_f)), sem="DVE",
            selfwait=True)
    c0_v = S_.val("DVE")

    lnw = emit_ln_transpose(S_, P, x_in)

    fig = P.f32[:, 6144:7680]
    f_sb, i_sb, g_sb = fig[:, 0:512], fig[:, 512:1024], fig[:, 1024:1536]
    lnx3 = P.lnx.ap().rearrange("p (kc tok) -> p kc tok", kc=8)

    def slot_ap(setid, s):
        if setid == 0:
            return P.psA[:, s * 256:(s + 1) * 256]
        if s < 2:
            return P.psA[:, 1536 + s * 256:1536 + (s + 1) * 256]
        return P.psB[:, (s - 2) * 256:(s - 1) * 256]

    act_set_done = [0, 0]
    dve_round_done = 0
    last_c_full = c0_v
    for t in range(16):
        for r in range(4):
            setid = r % 2
            xw = [("DVE", lnw["DVE"]), ("dW", w_all), ("ACT", act_set_done[setid])]
            first_mm = True
            for part in ("x", "c"):
                if part == "c" and t == 0:
                    continue
                if part == "c":
                    S_.wait("tensor", [("DVE", last_c_full)])
                is_last_part = (part == "c" or t == 0)
                for s in range(6):
                    gi, jj = s // 2, s % 2
                    cj = 2 * r + jj
                    oc = gi * 8 + cj
                    pj = slot_ap(setid, s)
                    for kc in range(8):
                        st = (part == "x" and s % 2 == 0 and kc == 0)
                        sp = (is_last_part and s % 2 == 1 and kc == 7)
                        if part == "x":
                            lhsT = Wx[:, kc * 3072 + oc * 128: kc * 3072 + (oc + 1) * 128]
                            rhs = lnx3[:, kc, t * 256:(t + 1) * 256]
                        else:
                            lhsT = Ux[:, kc * 3072 + oc * 128: kc * 3072 + (oc + 1) * 128]
                            rhs = c_b[:, kc * 256:(kc + 1) * 256]
                        S_.emit("tensor", (lambda e, pj=pj, lhsT=lhsT, rhs=rhs,
                                           st=st, sp=sp: e.matmul(
                            pj, lhsT, rhs, start=st, stop=sp)),
                            waits=xw if first_mm else (), sem="PE")
                        first_mm = False
            pe_round = S_.val("PE")

            aw = [("PE", pe_round), ("DVE", dve_round_done)]
            for s in range(6):
                gi, jj = s // 2, s % 2
                cj = 2 * r + jj
                oc = gi * 8 + cj
                pj = slot_ap(setid, s)
                dst = (f_sb, i_sb, g_sb)[gi][:, jj * 256:(jj + 1) * 256]
                func = AF.Tanh if gi == 2 else AF.Sigmoid
                bias_ap = (bias0 if t == 0 else b_f)[:, oc:oc + 1]
                S_.emit("scalar", (lambda e, dst=dst, pj=pj, func=func,
                                   bias_ap=bias_ap: e.activation(
                    out=dst, in_=pj, func=func, bias=bias_ap)),
                    waits=aw if s == 0 else (), sem="ACT")
            act_set_done[setid] = S_.val("ACT")
            act_v = S_.val("ACT")

            csl = slice(r * 512, (r + 1) * 512)
            S_.emit("vector", (lambda e: e.tensor_mul(
                i_sb, i_sb, g_sb)), waits=[("ACT", act_v)], sem="DVE")
            S_.emit("vector", (lambda e, csl=csl: e.tensor_mul(
                c_f[:, csl], c_f[:, csl], f_sb)), sem="DVE")
            S_.emit("vector", (lambda e, csl=csl: e.tensor_add(
                c_f[:, csl], c_f[:, csl], i_sb)), sem="DVE", selfwait=True)
            dve_round_done = S_.val("DVE")
        S_.emit("vector", (lambda e: e.tensor_copy(c_b, c_f)), sem="DVE",
                selfwait=True)
        last_c_full = S_.val("DVE")
    pe_recur_done = S_.val("PE")

    wo_b = Wx[:, 0:8192]
    uo_b = Wx[:, 8192:16384]
    S_.wait("gpsimd", [("PE", pe_recur_done)])
    for kc in range(8):
        S_.emit("gpsimd", (lambda e, kc=kc: e.dma_start(
            out=wo_b[:, kc * 1024:(kc + 1) * 1024],
            in_=d["ffn0_wo_t"][kc * 128:(kc + 1) * 128, :])), sem="dW", inc=16)
        S_.emit("gpsimd", (lambda e, kc=kc: e.dma_start(
            out=uo_b[:, kc * 1024:(kc + 1) * 1024],
            in_=d["ffn0_uo_t"][kc * 128:(kc + 1) * 128, :])), sem="dW", inc=16)
    w_tail = S_.val("dW")
    bo_row = P.sm[0:1, 1024:2048]
    S_.emit("gpsimd", (lambda e: e.dma_start(
        out=bo_row, in_=d["ffn0_bo"].unsqueeze(0))), sem="dW", inc=16)
    w_bo = S_.val("dW")

    if _DBG:
        S_.emit("sync", (lambda e: e.dma_start(
            out=d["dbgf"][0:128, 0:2048], in_=c_f)),
            waits=[("DVE", last_c_full)], sem="dX", inc=16)
        S_.emit("sync", (lambda e: e.dma_start(
            out=d["dbgb"][0:128, 0:4096], in_=P.lnx[:, 0:4096])),
            waits=[("DVE", lnw["DVE"])], sem="dX", inc=16)
        S_.emit("sync", (lambda e: e.dma_start(
            out=d["dbgf"][0:128, 2048:2304], in_=P.stat[:])),
            waits=[("DVE", lnw["DVE"])], sem="dX", inc=16)

    uw = [("dW", w_bo), ("DVE", last_c_full), ("ACT", S_.val("ACT"))]
    first = True
    for st in range(2):
        for nh in range(2):
            quad = P.psA[:, (st * 2 + nh) * 512:(st * 2 + nh + 1) * 512]
            for kc in range(8):
                lhsT = c_b[:, kc * 256 + st * 128: kc * 256 + (st + 1) * 128]
                rhs = uo_b[:, kc * 1024 + nh * 512: kc * 1024 + (nh + 1) * 512]
                S_.emit("tensor", (lambda e, quad=quad, lhsT=lhsT, rhs=rhs,
                                   kc=kc: e.matmul(
                    quad, lhsT, rhs, start=(kc == 0), stop=False)),
                    waits=uw if first else (), sem="PE")
                first = False
            rhs_b = bo_row[:, nh * 512:(nh + 1) * 512]
            S_.emit("tensor", (lambda e, quad=quad, rhs_b=rhs_b: e.matmul(
                quad, onesr, rhs_b, start=False, stop=True)), sem="PE")
    pe_u = S_.val("PE")
    u_sb = P.f32[:, 6144:8192]
    S_.emit("vector", (lambda e: e.tensor_copy(u_sb, P.psA[:])),
            waits=[("PE", pe_u)], sem="DVE")
    u_v = S_.val("DVE")

    XO = [P.f32[:, i * 1024:(i + 1) * 1024] for i in range(2)]
    XN = [P.f32[:, 2048:3072], P.f32[:, 4096:5120]]
    h_tmp = P.f32[:, 8192:9216]
    h2 = P.f32[:, 3 * 1024:4 * 1024]
    tt_done, stt_done, tanh_done, store_done = [], [], [], []
    for tau in range(32):
        st = tau % 2
        half = P.psA[:, (tau % 2) * 1024:(tau % 2 + 1) * 1024]
        lw_ = []
        if tau >= 2:
            lw_.append(("DVE", stt_done[tau - 2]))
        else:
            lw_ += [("ACT", lnw["ACT"]), ("DVE", lnw["DVE"])]
        xo = XO[tau % 2]
        ld = S_.emit("sync", (lambda e, xo=xo, tau=tau: e.dma_start(
            out=xo, in_=x_in[tau * 128:(tau + 1) * 128, :])),
            waits=lw_, sem="dX", inc=16)
        mw = [("DVE", u_v if tau < 2 else tt_done[tau - 2])]
        firstm = True
        for kc in range(8):
            lhsT = lnx3[:, kc, tau * 128:(tau + 1) * 128]
            for nh in range(2):
                rhs = wo_b[:, kc * 1024 + nh * 512: kc * 1024 + (nh + 1) * 512]
                out_ap = half[:, nh * 512:(nh + 1) * 512]
                S_.emit("tensor", (lambda e, out_ap=out_ap, lhsT=lhsT, rhs=rhs,
                                   kc=kc: e.matmul(
                    out_ap, lhsT, rhs, start=(kc == 0), stop=(kc == 7))),
                    waits=mw if firstm else (), sem="PE")
                firstm = False
        pe_t = S_.val("PE")
        tw = [("PE", pe_t)]
        if tau >= 1:
            tw.append(("ACT", tanh_done[tau - 1]))
        usl = u_sb[:, st * 1024:(st + 1) * 1024]
        S_.emit("vector", (lambda e, half=half, usl=usl: e.tensor_add(
            h_tmp, half, usl)), waits=tw, sem="DVE")
        tt_done.append(S_.val("DVE"))
        hw = [("DVE", tt_done[tau])]
        if tau >= 1:
            hw.append(("DVE", stt_done[tau - 1]))
        S_.emit("scalar", (lambda e: e.activation(
            out=h2, in_=h_tmp, func=AF.Tanh)), waits=hw, sem="ACT")
        tanh_done.append(S_.val("ACT"))
        xn = XN[tau % 2]
        sw = [("ACT", tanh_done[tau]), ("dX", ld)]
        if tau >= 2:
            sw.append(("dX", store_done[tau - 2]))
        S_.emit("vector", (lambda e, xn=xn, xo=xo: e.scalar_tensor_tensor(
            out=xn, in0=h2, scalar=0.5, in1=xo, op0=ALU.mult, op1=ALU.add)),
            waits=sw, sem="DVE")
        stt_done.append(S_.val("DVE"))
        S_.emit("sync", (lambda e, xn=xn, tau=tau: e.dma_start(
            out=x_out[tau * 128:(tau + 1) * 128, :], in_=xn)),
            waits=[("DVE", stt_done[tau])], sem="dX", inc=16)
        store_done.append(S_.val("dX"))
    return {"dX": S_.val("dX")}


def emit_bit_wquant(S_, P, d):
    nc = S_.nc
    smf, f32, wb = P.smf, P.f32, P.wb
    ld1 = S_.emit("gpsimd", (lambda e: e.dma_start(
        out=f32[:, 4096:8192], in_=d["bit_w1s"])), sem="dW", inc=16)
    ld2 = []
    for i in range(4):
        ld2.append(S_.emit("gpsimd", (lambda e, i=i: e.dma_start(
            out=f32[:, i * 1024:(i + 1) * 1024],
            in_=d["bit_w2s"][i * 128:(i + 1) * 128, :])), sem="dW", inc=16))
    s1p = smf[:, 86:90]
    s2p = smf[:, 91:95]
    s12 = smf[:, 96:98]
    for i in range(4):
        S_.emit("scalar", (lambda e, i=i: e.activation(
            out=f32[:, 8192:9216], in_=f32[:, 4096 + i * 1024:5120 + i * 1024],
            func=AF.Abs, accum_out=s1p[:, i:i + 1])),
            waits=[("dW", ld1)], sem="ACT", selfwait=True)
    S_.emit("vector", (lambda e: e.reduce_sum(
        s12[:, 0:1], s1p, axis=AXX)),
        waits=[("ACT", S_.val("ACT"))], sem="DVE")
    for i in range(4):
        S_.emit("scalar", (lambda e, i=i: e.activation(
            out=f32[:, 8192:9216], in_=f32[:, i * 1024:(i + 1) * 1024],
            func=AF.Abs, accum_out=s2p[:, i:i + 1])),
            waits=[("dW", ld2[i])], sem="ACT", selfwait=True)
    abs_done = S_.val("ACT")
    S_.emit("vector", (lambda e: e.reduce_sum(
        s12[:, 1:2], s2p, axis=AXX)), waits=[("ACT", abs_done)], sem="DVE")
    s12_v = S_.val("DVE")
    S_.emit("vector", (lambda e: e.tensor_copy(f32[:, 8192:8194], s12)),
            waits=[("DVE", s12_v)], sem="DVE", selfwait=True)
    pad_v = S_.val("DVE")
    for s in range(8):
        S_.emit("sync", (lambda e, s=s: e.dma_start(
            out=d["cc_in"][s * 512:s * 512 + 128, :],
            in_=f32[:, 8192:9216])),
            waits=[("DVE", pad_v)] if s == 0 else (), sem="dX", inc=16)
    S_.wait("gpsimd", [("dX", S_.val("dX"))])
    gp_cc = S_.emit("gpsimd", (lambda e: e.collective_compute(
        "AllToAll", ALU.bypass, RG8, ins=[d["cc_in"]], outs=[d["cc_out"]])),
        sem="GP")
    for c in range(2):
        for rk in range(8):
            S_.emit("gpsimd", (lambda e, c=c, rk=rk: e.dma_start(
                out=smf[:, 100 + 8 * c + rk:101 + 8 * c + rk],
                in_=d["cc_out"][rk * 512:rk * 512 + 128, c:c + 1],
                allow_slow_non_contiguous=True)),
                waits=[("GP", gp_cc)], sem="dW", inc=16)
    sums_v = S_.val("dW")
    if _DBG:
        for rk in range(8):
            S_.emit("gpsimd", (lambda e, rk=rk: e.dma_start(
                out=d["dbgf"][rk:rk + 1, 128:132],
                in_=d["cc_out"][rk * 512:rk * 512 + 1, 0:4])),
                sem="dW", inc=16)
    S_.emit("vector", (lambda e: e.reduce_sum(
        smf[:, 116:117], smf[:, 100:108], axis=AXX)),
        waits=[("dW", sums_v)], sem="DVE")
    S_.emit("vector", (lambda e: e.reduce_sum(
        smf[:, 117:118], smf[:, 108:116], axis=AXX)), sem="DVE")
    ones128 = smf[:, 128:256]
    S_.emit("vector", (lambda e: e.memset(ones128, 1.0)), sem="DVE")
    rdone = S_.val("DVE")
    S_.emit("tensor", (lambda e: e.matmul(
        P.psB[:, 0:2], ones128, smf[:, 116:118], start=True, stop=True)),
        waits=[("DVE", rdone)], sem="PE")
    pe_s = S_.val("PE")
    S_.emit("vector", (lambda e: e.tensor_scalar(
        out=smf[:, 64:65], in0=P.psB[:, 0:1],
        scalar1=1.0 / (1024.0 * 4096.0), scalar2=BIT_EPS,
        op0=ALU.mult, op1=ALU.max)), waits=[("PE", pe_s)], sem="DVE")
    S_.emit("vector", (lambda e: e.tensor_scalar(
        out=smf[:, 66:67], in0=P.psB[:, 1:2],
        scalar1=1.0 / (1024.0 * 4096.0), scalar2=BIT_EPS,
        op0=ALU.mult, op1=ALU.max)), sem="DVE")
    S_.emit("vector", (lambda e: e.reciprocal(
        smf[:, 65:66], smf[:, 64:65])), sem="DVE", selfwait=True)
    S_.emit("vector", (lambda e: e.reciprocal(
        smf[:, 67:68], smf[:, 66:67])), sem="DVE", selfwait=True)
    S_.emit("vector", (lambda e: e.tensor_copy(
        smf[:, 256:292], smf[:, 84:120])), sem="DVE", selfwait=True)
    ws_cols = S_.val("DVE")

    def quant(src, dst, inv_col, w):
        S_.emit("vector", (lambda e: e.tensor_scalar(
            out=src, in0=src, scalar1=inv_col, scalar2=MAGIC,
            op0=ALU.mult, op1=ALU.add)), waits=w, sem="DVE", selfwait=True)
        S_.emit("vector", (lambda e: e.tensor_scalar(
            out=src, in0=src, scalar1=MAGIC, scalar2=-1.0,
            op0=ALU.subtract, op1=ALU.max)), sem="DVE", selfwait=True)
        S_.emit("vector", (lambda e: e.tensor_scalar(
            out=dst, in0=src, scalar1=1.0, scalar2=None, op0=ALU.min)),
            sem="DVE", selfwait=True)
        return S_.val("DVE")

    q1 = quant(f32[:, 4096:8192], wb[:, 30976:35072], smf[:, 65:66],
               [("DVE", ws_cols)])
    for s in range(8):
        S_.emit("sync", (lambda e, s=s: e.dma_start(
            out=d["w1q_in"][s * 128:(s + 1) * 128, :],
            in_=wb[:, 30976:35072])),
            waits=[("DVE", q1)] if s == 0 else (), sem="dX", inc=16)
    st_prev = 0
    for i in range(4):
        qi = quant(f32[:, i * 1024:(i + 1) * 1024],
                   wb[:, 29952:30976], smf[:, 67:68],
                   [("dX", st_prev)] if i else ())
        for s in range(8):
            S_.emit("sync", (lambda e, i=i, s=s: e.dma_start(
                out=d["w2q_in"][s * 512 + i * 128:s * 512 + (i + 1) * 128, :],
                in_=wb[:, 29952:30976])),
                waits=[("DVE", qi)] if s == 0 else (), sem="dX", inc=16)
        st_prev = S_.val("dX")
    S_.wait("gpsimd", [("dX", S_.val("dX"))])
    S_.emit("gpsimd", (lambda e: e.collective_compute(
        "AllToAll", ALU.bypass, RG8, ins=[d["w1q_in"]], outs=[d["w1q_d"]])),
        sem="GP")
    S_.emit("gpsimd", (lambda e: e.collective_compute(
        "AllToAll", ALU.bypass, RG8, ins=[d["w2q_in"]], outs=[d["w2q_d"]])),
        sem="GP")
    S_.fence()


def emit_bitffn(S_, P, d, x_in, x_out):
    nc = S_.nc
    smf, f32, wb, stat = P.smf, P.f32, P.wb, P.stat
    for kc in range(8):
        S_.emit("gpsimd", (lambda e, kc=kc: e.dma_start(
            out=P.lnx[:, kc * 4096:(kc + 1) * 4096],
            in_=d["w1q_d"][kc * 128:(kc + 1) * 128, :])), sem="dW", inc=16)
    w1_v = S_.val("dW")
    ldb1 = S_.emit("gpsimd", (lambda e: e.dma_start(
        out=f32[0:1, 4096:8192], in_=d["bit_b1"].unsqueeze(0))),
        sem="dW", inc=16)
    ldb2 = S_.emit("gpsimd", (lambda e: e.dma_start(
        out=f32[0:1, 8192:9216], in_=d["bit_b2"].unsqueeze(0))),
        sem="dW", inc=16)
    b1row = wb[0:1, 24832:28928]
    b2row = wb[0:1, 28928:29952]
    b1lo = wb[0:1, 35136:39232]
    b2lo = wb[0:1, 39232:40256]
    S_.emit("vector", (lambda e: e.tensor_copy(b1row, f32[0:1, 4096:8192])),
            waits=[("dW", ldb1)], sem="DVE")
    S_.emit("vector", (lambda e: e.tensor_copy(b2row, f32[0:1, 8192:9216])),
            waits=[("dW", ldb2)], sem="DVE")
    S_.emit("vector", (lambda e: e.tensor_copy(f32[0:1, 0:4096], b1row)),
            sem="DVE", selfwait=True)
    S_.emit("vector", (lambda e: e.tensor_tensor(
        out=b1lo, in0=f32[0:1, 4096:8192], in1=f32[0:1, 0:4096],
        op=ALU.subtract)), sem="DVE", selfwait=True)
    S_.emit("vector", (lambda e: e.tensor_copy(f32[0:1, 0:1024], b2row)),
            sem="DVE", selfwait=True)
    S_.emit("vector", (lambda e: e.tensor_tensor(
        out=b2lo, in0=f32[0:1, 8192:9216], in1=f32[0:1, 0:1024],
        op=ALU.subtract)), sem="DVE", selfwait=True)
    brows_v = S_.val("DVE")

    rstd = stat[:, 0:32]
    mean = stat[:, 32:64]
    sums = stat[:, 64:96]
    sumsq = stat[:, 96:128]
    var = stat[:, 128:160]
    negmr = stat[:, 192:224]
    XT = [f32[:, 0:1024], f32[:, 1024:2048]]
    XLN = [f32[:, 2048:3072], f32[:, 3072:4096]]
    SQ = f32[:, 8192:9216]
    H = f32[:, 4096:8192]
    XQB = [wb[:, 0:1024], wb[:, 1024:2048]]
    XQF = [wb[:, 2048:3072], wb[:, 3072:4096]]
    XQ2 = [wb[:, 4096:8192], wb[:, 8192:12288]]
    IBR = [wb[0:1, 20480:20608], wb[0:1, 20608:20736]]
    IB2A = wb[0:1, 20736:24832]
    IBC = [wb[:, 35072:35073], wb[:, 35073:35074]]
    IB2C = [wb[:, 35074:35075], wb[:, 35075:35076]]
    eps_ap, eps_v = P.const_col(S_, LN_EPS)
    nid_ap, nid_v = P.const_col(S_, -1.0 / D)

    ln_apply, qa_done, mm_done, gelu_done, q2_done, st_done = \
        [], [], [], [], [], []
    ibr_evac, ib2_evac, fm_evac, ibrv_list, ib2rv_list = [], [], [], [], []
    for t in range(32):
        xt = XT[t % 2]
        xln = XLN[t % 2]
        w = [("dW", max(w1_v, ldb2))]
        if t < 2:
            w.append(("DVE", brows_v))
        if t >= 2:
            w.append(("ACT", ln_apply[t - 2]))
        vload = S_.emit("sync", (lambda e, xt=xt, t=t: e.dma_start(
            out=xt, in_=x_in[t * 128:(t + 1) * 128, :])), waits=w,
            sem="dX", inc=16)
        S_.emit("vector", (lambda e, xt=xt, t=t: e.reduce_sum(
            sums[:, t:t + 1], xt, axis=AXX)), waits=[("dX", vload)], sem="DVE")
        sq_w = [("dX", vload)]
        if t == 0:
            sq_w.append(("DVE", brows_v))
        if t >= 1:
            sq_w.append(("DVE", qa_done[t - 1]))
        S_.emit("scalar", (lambda e, xt=xt, t=t: e.activation(
            out=SQ, in_=xt, func=AF.Square, accum_out=sumsq[:, t:t + 1])),
            waits=sq_w, sem="ACT")
        sq_v = S_.val("ACT")
        S_.emit("vector", (lambda e, t=t: e.tensor_scalar(
            out=mean[:, t:t + 1], in0=sums[:, t:t + 1], scalar1=1.0 / D,
            scalar2=None, op0=ALU.mult)), sem="DVE", selfwait=True)
        S_.emit("vector", (lambda e, t=t: e.scalar_tensor_tensor(
            out=var[:, t:t + 1], in0=sums[:, t:t + 1], scalar=mean[:, t:t + 1],
            in1=sumsq[:, t:t + 1], op0=ALU.mult, op1=ALU.subtract)),
            waits=[("ACT", sq_v)], sem="DVE", selfwait=True)
        vfin = S_.val("DVE")
        S_.emit("scalar", (lambda e, t=t: e.activation(
            out=stat[:, 160 + t % 2:161 + t % 2], in_=var[:, t:t + 1],
            func=AF.Sqrt, scale=nid_ap, bias=eps_ap)),
            waits=[("DVE", max(vfin, eps_v, nid_v))], sem="ACT", selfwait=True)
        sstd = S_.val("ACT")
        S_.emit("vector", (lambda e, t=t: e.reciprocal(
            rstd[:, t:t + 1], stat[:, 160 + t % 2:161 + t % 2])),
            waits=[("ACT", sstd)], sem="DVE")
        S_.emit("vector", (lambda e, t=t: e.scalar_tensor_tensor(
            out=negmr[:, t:t + 1], in0=mean[:, t:t + 1], scalar=-1.0,
            in1=rstd[:, t:t + 1], op0=ALU.mult, op1=ALU.mult)),
            sem="DVE", selfwait=True)
        stats_v = S_.val("DVE")
        aw = [("DVE", stats_v)]
        if t >= 2:
            aw.append(("DVE", qa_done[t - 2]))
        S_.emit("scalar", (lambda e, xt=xt, xln=xln, t=t: e.activation(
            out=xln, in_=xt, func=AF.Identity, scale=rstd[:, t:t + 1],
            bias=negmr[:, t:t + 1])), waits=aw, sem="ACT", selfwait=True)
        ln_apply.append(S_.val("ACT"))
        S_.emit("vector", (lambda e, xln=xln: e.reduce_max(
            smf[:, 68:69], xln, axis=AXX, apply_absolute_value=True)),
            waits=[("ACT", ln_apply[t])], sem="DVE")
        S_.emit("vector", (lambda e: e.tensor_scalar(
            out=smf[:, 68:69], in0=smf[:, 68:69], scalar1=BIT_EPS,
            scalar2=None, op0=ALU.max)), sem="DVE", selfwait=True)
        S_.emit("vector", (lambda e: e.reciprocal(
            smf[:, 69:70], smf[:, 68:69])), sem="DVE", selfwait=True)
        S_.emit("vector", (lambda e: e.tensor_scalar(
            out=smf[:, 70:71], in0=smf[:, 69:70], scalar1=QB,
            scalar2=None, op0=ALU.mult)), sem="DVE", selfwait=True)
        S_.emit("vector", (lambda e: e.scalar_tensor_tensor(
            out=smf[:, 71:72], in0=smf[:, 70:71], scalar=smf[:, 65:66],
            in1=smf[:, 65:66], op0=ALU.mult, op1=ALU.bypass)),
            sem="DVE", selfwait=True)
        scev = smf[:, 72 + t % 2:73 + t % 2]
        gw = [("ACT", gelu_done[t - 2])] if t >= 2 else []
        S_.emit("vector", (lambda e, scev=scev: e.scalar_tensor_tensor(
            out=scev, in0=smf[:, 68:69], scalar=smf[:, 64:65],
            in1=smf[:, 64:65], op0=ALU.mult, op1=ALU.bypass)),
            waits=gw, sem="DVE", selfwait=True)
        S_.emit("vector", (lambda e, scev=scev: e.tensor_scalar(
            out=scev, in0=scev, scalar1=1.0 / QB, scalar2=None,
            op0=ALU.mult)), sem="DVE", selfwait=True)
        ibc = IBC[t % 2]
        ibw = [("PE", ibr_evac[t - 2])] if t >= 2 else []
        S_.emit("vector", (lambda e, ibc=ibc: e.tensor_copy(
            ibc, smf[:, 71:72])), waits=ibw, sem="DVE", selfwait=True)
        ibc_v = S_.val("DVE")
        xqb = XQB[t % 2]
        qw = []
        if t >= 2:
            qw.append(("PE", mm_done[t - 2]))
        S_.emit("vector", (lambda e, xln=xln: e.scalar_tensor_tensor(
            out=SQ, in0=xln, scalar=smf[:, 70:71], in1=xln,
            op0=ALU.mult, op1=ALU.bypass)),
            waits=qw + [("ACT", sq_v)], sem="DVE", selfwait=True)
        S_.emit("vector", (lambda e: e.tensor_scalar(
            out=SQ, in0=SQ, scalar1=MAGIC, scalar2=MAGIC,
            op0=ALU.add, op1=ALU.subtract)), sem="DVE", selfwait=True)
        S_.emit("vector", (lambda e, xqb=xqb: e.tensor_copy(xqb, SQ)),
                sem="DVE", selfwait=True)
        qa_done.append(S_.val("DVE"))
        pT = P.pT[t % 2]
        tw = [("DVE", qa_done[t])]
        if t >= 2:
            tw.append(("DVE", fm_evac[t - 2]))
        for kc in range(8):
            S_.emit("tensor", (lambda e, kc=kc, pT=pT, xqb=xqb: e.transpose(
                pT[:, kc * 128:(kc + 1) * 128], xqb[:, kc * 128:(kc + 1) * 128],
                P.ident)), waits=tw if kc == 0 else (), sem="PE")
        ibw2 = [("DVE", ibc_v)]
        if t >= 2:
            ibw2.append(("DVE", ibrv_list[t - 2]))
        S_.emit("tensor", (lambda e, ibc=ibc, t=t: e.matmul(
            P.psB[0:1, 256 + (t % 2) * 128:384 + (t % 2) * 128], ibc, P.ident,
            start=True, stop=True)), waits=ibw2, sem="PE")
        tv = S_.val("PE")
        ibr_evac.append(tv)
        xqf = XQF[t % 2]
        S_.emit("vector", (lambda e, pT=pT, xqf=xqf: e.tensor_copy(xqf, pT[:])),
                waits=[("PE", tv)], sem="DVE")
        fm_evac.append(S_.val("DVE"))
        ibr = IBR[t % 2]
        S_.emit("vector", (lambda e, ibr=ibr, t=t: e.tensor_copy(
            ibr, P.psB[0:1, 256 + (t % 2) * 128:384 + (t % 2) * 128])),
            waits=[("PE", mm_done[t - 2])] if t >= 2 else (),
            sem="DVE", selfwait=True)
        ibr_v = S_.val("DVE")
        ibrv_list.append(ibr_v)
        gds = []
        for c in range(4):
            ps = P.psA[:, (c % 2) * 1024:(c % 2 + 1) * 1024]
            mw = [("DVE", max(fm_evac[t], ibr_v, brows_v)), ("dW", w1_v)]
            if len(gds) >= 2:
                mw.append(("ACT", gds[-2]))
            elif t >= 1:
                mw.append(("ACT", gelu_done[t - 1]))
            first = True
            for oc in range(2):
                for kc in range(8):
                    S_.emit("tensor", (lambda e, ps=ps, oc=oc, kc=kc, c=c,
                                       xqf=xqf: e.matmul(
                        ps[:, oc * 512:(oc + 1) * 512],
                        xqf[:, kc * 128:(kc + 1) * 128],
                        P.lnx[:, kc * 4096 + c * 1024 + oc * 512:
                              kc * 4096 + c * 1024 + (oc + 1) * 512],
                        start=(kc == 0), stop=False)),
                        waits=mw if first else (), sem="PE")
                    first = False
                S_.emit("tensor", (lambda e, ps=ps, oc=oc, c=c, ibr=ibr:
                                   e.matmul(
                    ps[:, oc * 512:(oc + 1) * 512], ibr,
                    b1row[:, c * 1024 + oc * 512:c * 1024 + (oc + 1) * 512],
                    start=False, stop=False)), sem="PE")
                S_.emit("tensor", (lambda e, ps=ps, oc=oc, c=c, ibr=ibr:
                                   e.matmul(
                    ps[:, oc * 512:(oc + 1) * 512], ibr,
                    b1lo[:, c * 1024 + oc * 512:c * 1024 + (oc + 1) * 512],
                    start=False, stop=True)), sem="PE")
            pe_c = S_.val("PE")
            hw = [("PE", pe_c)]
            if t >= 1 and c == 0:
                hw.append(("DVE", q2_done[t - 1]))
            S_.emit("scalar", (lambda e, ps=ps, c=c, scev=scev: e.activation(
                out=H[:, c * 1024:(c + 1) * 1024], in_=ps, func=AF.Gelu,
                scale=scev)), waits=hw, sem="ACT", selfwait=True)
            gds.append(S_.val("ACT"))
        mm_done.append(S_.val("PE"))
        gelu_done.append(S_.val("ACT"))
        S_.emit("vector", (lambda e: e.reduce_max(
            smf[:, 74:75], H, axis=AXX, apply_absolute_value=True)),
            waits=[("ACT", gelu_done[t])], sem="DVE")
        S_.emit("vector", (lambda e: e.tensor_scalar(
            out=smf[:, 74:75], in0=smf[:, 74:75], scalar1=BIT_EPS,
            scalar2=None, op0=ALU.max)), sem="DVE", selfwait=True)
        S_.emit("vector", (lambda e: e.reciprocal(
            smf[:, 75:76], smf[:, 74:75])), sem="DVE", selfwait=True)
        S_.emit("vector", (lambda e: e.tensor_scalar(
            out=smf[:, 76:77], in0=smf[:, 75:76], scalar1=QB,
            scalar2=None, op0=ALU.mult)), sem="DVE", selfwait=True)
        S_.emit("vector", (lambda e: e.scalar_tensor_tensor(
            out=smf[:, 77:78], in0=smf[:, 76:77], scalar=smf[:, 67:68],
            in1=smf[:, 67:68], op0=ALU.mult, op1=ALU.bypass)),
            sem="DVE", selfwait=True)
        S_.emit("vector", (lambda e, t=t: e.scalar_tensor_tensor(
            out=smf[:, 80 + t:81 + t], in0=smf[:, 74:75], scalar=smf[:, 66:67],
            in1=smf[:, 66:67], op0=ALU.mult, op1=ALU.bypass)),
            sem="DVE", selfwait=True)
        S_.emit("vector", (lambda e, t=t: e.tensor_scalar(
            out=smf[:, 80 + t:81 + t], in0=smf[:, 80 + t:81 + t],
            scalar1=0.5 / QB, scalar2=None, op0=ALU.mult)),
            sem="DVE", selfwait=True)
        ib2c = IB2C[t % 2]
        ib2w = [("PE", ib2_evac[t - 2])] if t >= 2 else []
        S_.emit("vector", (lambda e, ib2c=ib2c: e.tensor_copy(
            ib2c, smf[:, 77:78])), waits=ib2w, sem="DVE", selfwait=True)
        ib2c_v = S_.val("DVE")
        xq2 = XQ2[t % 2]
        q2w = [("dX", st_done[t - 2])] if t >= 2 else []
        S_.emit("vector", (lambda e: e.scalar_tensor_tensor(
            out=H, in0=H, scalar=smf[:, 76:77], in1=H,
            op0=ALU.mult, op1=ALU.bypass)), waits=q2w, sem="DVE",
            selfwait=True)
        S_.emit("vector", (lambda e: e.tensor_scalar(
            out=H, in0=H, scalar1=MAGIC, scalar2=MAGIC,
            op0=ALU.add, op1=ALU.subtract)), sem="DVE", selfwait=True)
        S_.emit("vector", (lambda e, xq2=xq2: e.tensor_copy(xq2, H)),
                sem="DVE", selfwait=True)
        q2_done.append(S_.val("DVE"))
        ib2w2 = [("DVE", ib2c_v)]
        if t >= 2:
            ib2w2.append(("DVE", ib2rv_list[t - 2]))
        S_.emit("tensor", (lambda e, ib2c=ib2c, t=t: e.matmul(
            P.psB[0:1, 768 + (t % 2) * 128:896 + (t % 2) * 128], ib2c, P.ident,
            start=True, stop=True)), waits=ib2w2, sem="PE")
        tv2 = S_.val("PE")
        ib2_evac.append(tv2)
        S_.emit("vector", (lambda e, t=t: e.tensor_copy(
            IB2A[:, t * 128:(t + 1) * 128],
            P.psB[0:1, 768 + (t % 2) * 128:896 + (t % 2) * 128])),
            waits=[("PE", tv2)], sem="DVE")
        ib2rv_list.append(S_.val("DVE"))
        S_.emit("sync", (lambda e, xq2=xq2, t=t: e.dma_start(
            out=d["xq2_d"][t * 128:(t + 1) * 128, :], in_=xq2)),
            waits=[("DVE", q2_done[t])], sem="dX", inc=16)
        st_done.append(S_.val("dX"))
    walk1_pe = S_.val("PE")
    walk1_dx = S_.val("dX")
    walk1_act = S_.val("ACT")

    S_.wait("gpsimd", [("PE", walk1_pe)])
    for kc in range(32):
        S_.emit("gpsimd", (lambda e, kc=kc: e.dma_start(
            out=P.lnx[:, kc * 1024:(kc + 1) * 1024],
            in_=d["w2q_d"][kc * 128:(kc + 1) * 128, :])), sem="dW", inc=16)
    w2_v = S_.val("dW")
    XQ2W = [wb[:, 12288:16384], wb[:, 16384:20480]]
    XQ2F = [wb[:, 0:4096], wb[:, 4096:8192]]
    XO = [f32[:, 0:1024], f32[:, 1024:2048]]
    XN = [f32[:, 2048:3072], f32[:, 3072:4096]]
    ld_done, tr_done, mm2_done, res_done, out_done = [], [], [], [], []
    for t in range(32):
        xq2w = XQ2W[t % 2]
        lw = [("dX", walk1_dx)]
        if t >= 2:
            lw.append(("PE", tr_done[t - 2]))
        ld = S_.emit("sync", (lambda e, xq2w=xq2w, t=t: e.dma_start(
            out=xq2w, in_=d["xq2_d"][t * 128:(t + 1) * 128, :])),
            waits=lw, sem="dX", inc=16)
        ld_done.append(ld)
        xo = XO[t % 2]
        ow = ([("DVE", res_done[t - 2])] if t >= 2
              else [("ACT", walk1_act)])
        ldx = S_.emit("sync", (lambda e, xo=xo, t=t: e.dma_start(
            out=xo, in_=x_in[t * 128:(t + 1) * 128, :])),
            waits=ow, sem="dX", inc=16)
        xq2f = XQ2F[t % 2]
        if t == 0:
            pt_evacs = []
        for r in range(4):
            g = t * 4 + r
            pT = P.pT[g % 2]
            tw = [("dX", ld)]
            if g >= 2:
                tw.append(("DVE", pt_evacs[g - 2]))
            else:
                tw.append(("DVE", fm_evac[-1]))
            for kc in range(8):
                S_.emit("tensor", (lambda e, pT=pT, r=r, kc=kc, xq2w=xq2w:
                                   e.transpose(
                    pT[:, kc * 128:(kc + 1) * 128],
                    xq2w[:, r * 1024 + kc * 128:r * 1024 + (kc + 1) * 128],
                    P.ident)), waits=tw if kc == 0 else (), sem="PE")
            ptv = S_.val("PE")
            S_.emit("vector", (lambda e, pT=pT, r=r, xq2f=xq2f: e.tensor_copy(
                xq2f[:, r * 1024:(r + 1) * 1024], pT[:])),
                waits=[("PE", ptv)], sem="DVE")
            pt_evacs.append(S_.val("DVE"))
        tr_evs = pt_evacs[-4:]
        tr_done.append(S_.val("PE"))
        ps = P.psA[:, (t % 2) * 1024:(t % 2 + 1) * 1024]
        mw = [("DVE", tr_evs[3]), ("dW", w2_v)]
        if t >= 2:
            mw.append(("DVE", res_done[t - 2]))
        first = True
        for oc in range(2):
            for kc in range(32):
                S_.emit("tensor", (lambda e, ps=ps, oc=oc, kc=kc, xq2f=xq2f:
                                   e.matmul(
                    ps[:, oc * 512:(oc + 1) * 512],
                    xq2f[:, kc * 128:(kc + 1) * 128],
                    P.lnx[:, kc * 1024 + oc * 512:kc * 1024 + (oc + 1) * 512],
                    start=(kc == 0), stop=False)),
                    waits=mw if first else (), sem="PE")
                first = False
            S_.emit("tensor", (lambda e, ps=ps, oc=oc, t=t: e.matmul(
                ps[:, oc * 512:(oc + 1) * 512], IB2A[:, t * 128:(t + 1) * 128],
                b2row[:, oc * 512:(oc + 1) * 512],
                start=False, stop=False)), sem="PE")
            S_.emit("tensor", (lambda e, ps=ps, oc=oc, t=t: e.matmul(
                ps[:, oc * 512:(oc + 1) * 512], IB2A[:, t * 128:(t + 1) * 128],
                b2lo[:, oc * 512:(oc + 1) * 512],
                start=False, stop=True)), sem="PE")
        mm2_done.append(S_.val("PE"))
        xn = XN[t % 2]
        rw = [("PE", mm2_done[t]), ("dX", ldx)]
        if t >= 2:
            rw.append(("dX", out_done[t - 2]))
        S_.emit("vector", (lambda e, ps=ps, xn=xn, xo=xo, t=t:
                           e.scalar_tensor_tensor(
            out=xn, in0=ps, scalar=smf[:, 80 + t:81 + t], in1=xo,
            op0=ALU.mult, op1=ALU.add)), waits=rw, sem="DVE")
        res_done.append(S_.val("DVE"))
        S_.emit("sync", (lambda e, xn=xn, t=t: e.dma_start(
            out=x_out[t * 128:(t + 1) * 128, :], in_=xn)),
            waits=[("DVE", res_done[t])], sem="dX", inc=16)
        out_done.append(S_.val("dX"))
    if _DBG:
        S_.emit("sync", (lambda e: e.dma_start(
            out=d["dbgf"][:, 0:64], in_=smf[:, 64:128])),
            waits=[("DVE", S_.val("DVE"))], sem="dX", inc=16)
        S_.emit("sync", (lambda e: e.dma_start(
            out=d["dbgf"][:, 64:128], in_=smf[:, 256:320])),
            sem="dX", inc=16)
        S_.emit("sync", (lambda e: e.dma_start(
            out=d["dbgb"], in_=d["w1q_d"][128:256, :])), sem="dX", inc=16)
        S_.emit("sync", (lambda e: e.dma_start(
            out=d["dbgb2"], in_=XQ2[1])), sem="dX", inc=16)
    S_.fence()


def bt_tile(bl, ch):
    return (ch // 2) * 4 + bl * 2 + (ch % 2)


def emit_attn(S_, P, d, x_in, x_out):
    nc = S_.nc
    smf, f32, wb, sm = P.smf, P.f32, P.wb, P.sm
    S_.fence()
    WQ = wb[:, 0:8192]
    WK = wb[:, 8192:10240]
    WV = wb[:, 10240:12288]
    WO = wb[:, 12288:20480]
    for kc in range(8):
        S_.emit("gpsimd", (lambda e, kc=kc: e.dma_start(
            out=WQ[:, kc * 1024:(kc + 1) * 1024],
            in_=d["attn_wq"][kc * 128:(kc + 1) * 128, :])), sem="dW", inc=16)
        S_.emit("gpsimd", (lambda e, kc=kc: e.dma_start(
            out=WK[:, kc * 256:(kc + 1) * 256],
            in_=d["attn_wk"][kc * 128:(kc + 1) * 128, :])), sem="dW", inc=16)
        S_.emit("gpsimd", (lambda e, kc=kc: e.dma_start(
            out=WV[:, kc * 256:(kc + 1) * 256],
            in_=d["attn_wv"][kc * 128:(kc + 1) * 128, :])), sem="dW", inc=16)
        S_.emit("gpsimd", (lambda e, kc=kc: e.dma_start(
            out=WO[:, kc * 1024:(kc + 1) * 1024],
            in_=d["attn_wo"][kc * 128:(kc + 1) * 128, :])), sem="dW", inc=16)
    w_all = S_.val("dW")
    QF = wb[:, 20480:36864]
    KF = wb[:, 36864:40960]
    VT = wb[:, 40960:45056]
    PTE = [wb[:, 45056:45568], wb[:, 45568:46080]]
    OTM = wb[:, 46080:47104]
    OFM = wb[:, 47104:48128]
    QTM = wb[:, 48128:49152]
    PSB = [wb[:, 48128:48576], wb[:, 48576:49024]]
    KTM = sm[:, 2048:2304]
    MASK = sm[:, 1024:1472]
    qf3 = QF.rearrange("p (kc tok) -> p kc tok", kc=8)
    kf3 = KF.rearrange("p (kc tok) -> p kc tok", kc=2)

    lnw = emit_ln_transpose(S_, P, x_in)
    lnx3 = P.lnx.ap().rearrange("p (kc tok) -> p kc tok", kc=8)

    ROPE = f32[:, 8192:8256]
    SQA = f32[:, 4096:5120]
    SQK = f32[:, 7168:7424]
    QN = f32[:, 5120:6144]
    KN = f32[:, 7424:7680]
    T1 = f32[:, 6144:6656]
    T2 = f32[:, 6656:7168]
    T1K = f32[:, 7680:7808]
    T2K = f32[:, 7808:7936]
    XO = [f32[:, 0:1024], f32[:, 1024:2048]]
    XN = [f32[:, 2048:3072], f32[:, 3072:4096]]
    ssq = smf[:, 320:336]
    rstdq = smf[:, 336:352]
    ssqk = smf[:, 352:356]
    rstdk = smf[:, 356:360]
    den = smf[:, 360:376]
    rden = smf[:, 376:392]
    stdq = smf[:, 424:440]
    stdk = smf[:, 440:444]
    c_e64, v_e64 = P.const_col(S_, 64.0 * 1.1920929e-07)
    c_i64, v_i64 = P.const_col(S_, 1.0 / 64.0)
    c_eps, v_eps = P.const_col(S_, 1.1920929e-07)
    cv = max(v_e64, v_i64, v_eps)

    ptu = 0
    pt_evac = []

    def pt_take(waits):
        nonlocal ptu
        w = list(waits)
        if ptu >= 2:
            w.append(("DVE", pt_evac[ptu - 2]))
        ptu += 1
        return P.pT[(ptu - 1) % 2], w

    rope_used = 0
    psB_free = 0
    psA_kv_free = 0
    av_free = 0
    mask_used = lnw["PE"]
    res_done = {}
    out_done = {}

    for bl in range(2):
        for ch in range(16):
            ti = bt_tile(bl, ch)
            rl = S_.emit("sync", (lambda e, ch=ch: e.dma_start(
                out=ROPE, in_=d["rope_cs"][ch * 128:(ch + 1) * 128, :])),
                waits=[("DVE", rope_used)], sem="dX", inc=16)
            mw = [("DVE", max(lnw["DVE"], psB_free)), ("dW", w_all)]
            for kc in range(8):
                for nb in range(2):
                    S_.emit("tensor", (lambda e, kc=kc, nb=nb, ti=ti: e.matmul(
                        P.psB[:, nb * 512:(nb + 1) * 512],
                        lnx3[:, kc, ti * 128:(ti + 1) * 128],
                        WQ[:, kc * 1024 + nb * 512:kc * 1024 + (nb + 1) * 512],
                        start=(kc == 0), stop=(kc == 7))),
                        waits=mw if (kc == 0 and nb == 0) else (), sem="PE")
            pe_q = S_.val("PE")
            S_.emit("scalar", (lambda e: e.activation(
                out=SQA, in_=P.psB[:], func=AF.Square)),
                waits=[("PE", pe_q)], sem="ACT", selfwait=True)
            sq_v = S_.val("ACT")
            S_.emit("vector", (lambda e: e.reduce_sum(
                ssq, SQA.rearrange("p (h d) -> p h d", h=16), axis=AXX)),
                waits=[("ACT", sq_v)], sem="DVE", selfwait=True)
            ssq_v = S_.val("DVE")
            S_.emit("scalar", (lambda e: e.activation(
                out=stdq, in_=ssq, func=AF.Sqrt, bias=c_e64)),
                waits=[("DVE", max(ssq_v, cv))], sem="ACT", selfwait=True)
            rq_v = S_.val("ACT")
            S_.emit("vector", (lambda e: e.reciprocal(rstdq, stdq)),
                    waits=[("ACT", rq_v)], sem="DVE", selfwait=True)
            S_.emit("vector", (lambda e: e.tensor_tensor(
                out=QN.rearrange("p (h d) -> p h d", h=16),
                in0=P.psB.ap().rearrange("p (h d) -> p h d", h=16),
                in1=rstdq.unsqueeze(2).broadcast_to([128, 16, 64]),
                op=ALU.mult)), waits=[("ACT", rq_v)], sem="DVE",
                selfwait=True)
            psB_free = S_.val("DVE")
            qn3 = QN.rearrange("p (h d) -> p h d", h=16)
            qtm3 = QTM.rearrange("p (h d) -> p h d", h=16)
            cbc = ROPE[:, 0:32].unsqueeze(1).broadcast_to([128, 16, 32])
            sbc = ROPE[:, 32:64].unsqueeze(1).broadcast_to([128, 16, 32])
            t13 = T1.rearrange("p (h d) -> p h d", h=16)
            t23 = T2.rearrange("p (h d) -> p h d", h=16)
            S_.emit("vector", (lambda e: e.tensor_tensor(
                out=t13, in0=qn3[:, :, 0:32], in1=cbc, op=ALU.mult)),
                waits=[("dX", rl)], sem="DVE", selfwait=True)
            S_.emit("vector", (lambda e: e.tensor_tensor(
                out=t23, in0=qn3[:, :, 32:64], in1=sbc, op=ALU.mult)),
                sem="DVE", selfwait=True)
            S_.emit("vector", (lambda e: e.tensor_tensor(
                out=qtm3[:, :, 0:32], in0=t13, in1=t23, op=ALU.add)),
                sem="DVE", selfwait=True)
            S_.emit("vector", (lambda e: e.tensor_tensor(
                out=t13, in0=qn3[:, :, 32:64], in1=cbc, op=ALU.mult)),
                sem="DVE", selfwait=True)
            S_.emit("vector", (lambda e: e.tensor_tensor(
                out=t23, in0=qn3[:, :, 0:32], in1=sbc, op=ALU.mult)),
                sem="DVE", selfwait=True)
            S_.emit("vector", (lambda e: e.tensor_tensor(
                out=qtm3[:, :, 32:64], in0=t13, in1=t23, op=ALU.subtract)),
                sem="DVE", selfwait=True)
            qtm_v = S_.val("DVE")
            pT, tw = pt_take([("DVE", qtm_v)])
            for kc in range(8):
                S_.emit("tensor", (lambda e, kc=kc, pT=pT: e.transpose(
                    pT[:, kc * 128:(kc + 1) * 128],
                    QTM[:, kc * 128:(kc + 1) * 128], P.ident)),
                    waits=tw if kc == 0 else (), sem="PE")
            tv = S_.val("PE")
            S_.emit("vector", (lambda e, pT=pT, ch=ch: e.tensor_copy(
                qf3[:, :, ch * 128:(ch + 1) * 128],
                pT[:].rearrange("p (kc c) -> p kc c", kc=8))),
                waits=[("PE", tv)], sem="DVE")
            pt_evac.append(S_.val("DVE"))

            kw = [("DVE", psA_kv_free), ("dW", w_all)]
            for kc in range(8):
                S_.emit("tensor", (lambda e, kc=kc, ti=ti: e.matmul(
                    P.psA[:, 0:256], lnx3[:, kc, ti * 128:(ti + 1) * 128],
                    WK[:, kc * 256:(kc + 1) * 256],
                    start=(kc == 0), stop=(kc == 7))),
                    waits=kw if kc == 0 else (), sem="PE")
            pe_k = S_.val("PE")
            S_.emit("scalar", (lambda e: e.activation(
                out=SQK, in_=P.psA[:, 0:256], func=AF.Square)),
                waits=[("PE", pe_k)], sem="ACT", selfwait=True)
            sqk_v = S_.val("ACT")
            S_.emit("vector", (lambda e: e.reduce_sum(
                ssqk, SQK.rearrange("p (h d) -> p h d", h=4), axis=AXX)),
                waits=[("ACT", sqk_v)], sem="DVE", selfwait=True)
            ssqk_v = S_.val("DVE")
            S_.emit("scalar", (lambda e: e.activation(
                out=stdk, in_=ssqk, func=AF.Sqrt, scale=c_i64, bias=c_eps)),
                waits=[("DVE", ssqk_v)], sem="ACT", selfwait=True)
            rk_v = S_.val("ACT")
            S_.emit("vector", (lambda e: e.reciprocal(rstdk, stdk)),
                    waits=[("ACT", rk_v)], sem="DVE", selfwait=True)
            S_.emit("vector", (lambda e: e.tensor_tensor(
                out=KN.rearrange("p (h d) -> p h d", h=4),
                in0=P.psA[:, 0:256].rearrange("p (h d) -> p h d", h=4),
                in1=rstdk.unsqueeze(2).broadcast_to([128, 4, 64]),
                op=ALU.mult)), waits=[("ACT", rk_v)], sem="DVE",
                selfwait=True)
            kn_free = S_.val("DVE")
            kn3 = KN.rearrange("p (h d) -> p h d", h=4)
            ktm3 = KTM.rearrange("p (h d) -> p h d", h=4)
            cb4 = ROPE[:, 0:32].unsqueeze(1).broadcast_to([128, 4, 32])
            sb4 = ROPE[:, 32:64].unsqueeze(1).broadcast_to([128, 4, 32])
            t13k = T1K.rearrange("p (h d) -> p h d", h=4)
            t23k = T2K.rearrange("p (h d) -> p h d", h=4)
            S_.emit("vector", (lambda e: e.tensor_tensor(
                out=t13k, in0=kn3[:, :, 0:32], in1=cb4, op=ALU.mult)),
                sem="DVE", selfwait=True)
            S_.emit("vector", (lambda e: e.tensor_tensor(
                out=t23k, in0=kn3[:, :, 32:64], in1=sb4, op=ALU.mult)),
                sem="DVE", selfwait=True)
            S_.emit("vector", (lambda e: e.tensor_tensor(
                out=ktm3[:, :, 0:32], in0=t13k, in1=t23k, op=ALU.add)),
                sem="DVE", selfwait=True)
            S_.emit("vector", (lambda e: e.tensor_tensor(
                out=t13k, in0=kn3[:, :, 32:64], in1=cb4, op=ALU.mult)),
                sem="DVE", selfwait=True)
            S_.emit("vector", (lambda e: e.tensor_tensor(
                out=t23k, in0=kn3[:, :, 0:32], in1=sb4, op=ALU.mult)),
                sem="DVE", selfwait=True)
            S_.emit("vector", (lambda e: e.tensor_tensor(
                out=ktm3[:, :, 32:64], in0=t13k, in1=t23k, op=ALU.subtract)),
                sem="DVE", selfwait=True)
            rope_used = S_.val("DVE")
            ktm_v = rope_used
            pTk, tkw = pt_take([("DVE", ktm_v)])
            for kc in range(2):
                S_.emit("tensor", (lambda e, kc=kc, pTk=pTk: e.transpose(
                    pTk[:, kc * 128:(kc + 1) * 128],
                    KTM[:, kc * 128:(kc + 1) * 128], P.ident)),
                    waits=tkw if kc == 0 else (), sem="PE")
            tvk = S_.val("PE")
            S_.emit("vector", (lambda e, pTk=pTk, ch=ch: e.tensor_copy(
                kf3[:, :, ch * 128:(ch + 1) * 128],
                pTk[:, 0:256].rearrange("p (kc c) -> p kc c", kc=2))),
                waits=[("PE", tvk)], sem="DVE")
            pt_evac.append(S_.val("DVE"))

            vw = [("DVE", psA_kv_free), ("dW", w_all)]
            for kc in range(8):
                S_.emit("tensor", (lambda e, kc=kc, ti=ti: e.matmul(
                    P.psA[:, 512:768], lnx3[:, kc, ti * 128:(ti + 1) * 128],
                    WV[:, kc * 256:(kc + 1) * 256],
                    start=(kc == 0), stop=(kc == 7))),
                    waits=vw if kc == 0 else (), sem="PE")
            pe_v = S_.val("PE")
            S_.emit("vector", (lambda e, ch=ch: e.tensor_copy(
                VT[:, ch * 256:(ch + 1) * 256], P.psA[:, 512:768])),
                waits=[("PE", pe_v)], sem="DVE")
            psA_kv_free = max(kn_free, S_.val("DVE"))
            if _DBG and bl == 0 and ch == 4:
                S_.emit("sync", (lambda e: e.dma_start(
                    out=d["dbgb"][:, 2048:3072], in_=QTM)),
                    waits=[("DVE", qtm_v)], sem="dX", inc=16)
                S_.emit("sync", (lambda e: e.dma_start(
                    out=d["dbgb"][:, 3072:3328], in_=KTM)),
                    waits=[("DVE", rope_used)], sem="dX", inc=16)
                S_.emit("sync", (lambda e, ch=ch: e.dma_start(
                    out=d["dbgb"][:, 3328:3584],
                    in_=VT[:, ch * 256:(ch + 1) * 256])),
                    waits=[("DVE", S_.val("DVE"))], sem="dX", inc=16)
                S_.wait("vector", [("dX", S_.val("dX"))])
        prod_all = S_.val("DVE")

        exp_done = {}
        tr_done = {}
        evac_done = {}
        av_done = {}
        for ch in range(16):
            ti = bt_tile(bl, ch)
            ncol = 448 if ch >= 3 else (ch + 1) * 128
            nblk = 4 if ch >= 3 else ch + 1
            ml = S_.emit("sync", (lambda e, ch=ch, ncol=ncol: e.dma_start(
                out=MASK[:, 0:ncol],
                in_=d["attn_mask"][ch * 128:(ch + 1) * 128, 0:ncol])),
                waits=[("PE", mask_used)], sem="dX", inc=16)
            xw = []
            if (bl, ch - 2) in res_done:
                xw.append(("DVE", res_done[(bl, ch - 2)]))
            xl = S_.emit("sync", (lambda e, ti=ti, xo=XO[ch % 2]: e.dma_start(
                out=xo, in_=x_in[ti * 128:(ti + 1) * 128, :])),
                waits=xw, sem="dX", inc=16)
            for h in range(16):
                kvh = h // 4
                kch = kvh // 2
                koff = (kvh % 2) * 64
                qch = QPOS[h] // 2
                qoff = (QPOS[h] % 2) * 64
                assert qoff == koff
                sb = P.psA[:, (h % 2) * 512:(h % 2) * 512 + 512]
                lq = QF[qoff:qoff + 64,
                        qch * 2048 + ch * 128:qch * 2048 + (ch + 1) * 128]
                smw = [("DVE", prod_all), ("dX", ml)]
                if h >= 2:
                    smw.append(("ACT", exp_done[h - 2]))
                elif ch >= 1:
                    smw.append(("ACT", exp_done[14 + h]))
                if h == 0:
                    smw.append(("DVE", av_free))
                if ch >= 3:
                    S_.emit("tensor", (lambda e, sb=sb, lq=lq, kch=kch,
                                       koff=koff, ch=ch: e.matmul(
                        sb[:, 64:448], lq,
                        KF[koff:koff + 64, kch * 2048 + (ch - 2) * 128:
                           kch * 2048 + (ch + 1) * 128],
                        start=True, stop=False)), waits=smw, sem="PE")
                    S_.emit("tensor", (lambda e, sb=sb, lq=lq, kch=kch,
                                       koff=koff: e.matmul(
                        sb[:, 0:64], lq,
                        KF[koff:koff + 64, kch * 2048:kch * 2048 + 64],
                        start=False, stop=False)), sem="PE")
                else:
                    S_.emit("tensor", (lambda e, sb=sb, lq=lq, kch=kch,
                                       koff=koff, ncol=ncol: e.matmul(
                        sb[:, 0:ncol], lq,
                        KF[koff:koff + 64, kch * 2048:kch * 2048 + ncol],
                        start=True, stop=False)), waits=smw, sem="PE")
                S_.emit("tensor", (lambda e, sb=sb, ncol=ncol: e.matmul(
                    sb[:, 0:ncol], P.ident, MASK[:, 0:ncol],
                    start=False, stop=True)), sem="PE")
                sc_pe = S_.val("PE")
                ew = [("PE", sc_pe)]
                if h >= 2:
                    ew.append(("PE", tr_done[h - 2]))
                elif ch >= 1:
                    ew.append(("PE", tr_done[14 + h]))
                S_.emit("scalar", (lambda e, sb=sb, h=h, ncol=ncol:
                                   e.activation(
                    out=PSB[h % 2][:, 0:ncol], in_=sb[:, 0:ncol], func=AF.Exp,
                    accum_out=den[:, h:h + 1])), waits=ew, sem="ACT",
                    selfwait=True)
                exp_done[h] = S_.val("ACT")
                pTp, tw = pt_take([("ACT", exp_done[h])])
                blks = ([(0, 64)] + [(64 + j * 128, 128) for j in range(3)]
                        if ch >= 3 else
                        [(j * 128, 128) for j in range(nblk)])
                first = True
                for j, (c0, w_) in enumerate(blks):
                    S_.emit("tensor", (lambda e, pTp=pTp, j=j, c0=c0, w_=w_,
                                       h=h: e.transpose(
                        pTp[0:w_, j * 128:(j + 1) * 128],
                        PSB[h % 2][:, c0:c0 + w_], P.ident)),
                        waits=tw if first else (), sem="PE")
                    first = False
                tr_done[h] = S_.val("PE")
                vw2 = [("PE", tr_done[h])]
                if h >= 2:
                    vw2.append(("PE", av_done[h - 2]))
                elif ch >= 1:
                    vw2.append(("PE", av_done[14 + h]))
                S_.emit("vector", (lambda e, pTp=pTp, h=h, nblk=nblk:
                                   e.tensor_copy(
                    PTE[h % 2][:, 0:nblk * 128], pTp[:, 0:nblk * 128])),
                    waits=vw2, sem="DVE")
                evac_done[h] = S_.val("DVE")
                pt_evac.append(evac_done[h])
                av = P.psA[:, 1024 + h * 64:1024 + (h + 1) * 64]
                avw = [("DVE", evac_done[h])]
                first = True
                for j in range(nblk):
                    if ch >= 3:
                        if j == 0:
                            rv = VT[0:64, kvh * 64:kvh * 64 + 64]
                            lp = PTE[h % 2][0:64, 0:128]
                        else:
                            kt = ch - 3 + j
                            rv = VT[:, kt * 256 + kvh * 64:
                                    kt * 256 + kvh * 64 + 64]
                            lp = PTE[h % 2][:, j * 128:(j + 1) * 128]
                    else:
                        rv = VT[:, j * 256 + kvh * 64:j * 256 + kvh * 64 + 64]
                        lp = PTE[h % 2][:, j * 128:(j + 1) * 128]
                    S_.emit("tensor", (lambda e, av=av, lp=lp, rv=rv, j=j,
                                       nblk=nblk: e.matmul(
                        av, lp, rv, start=(j == 0), stop=(j == nblk - 1))),
                        waits=avw if first else (), sem="PE")
                    first = False
                av_done[h] = S_.val("PE")
            for h in range(16):
                exp_done[h - 16] = exp_done[h]
                tr_done[h - 16] = tr_done[h]
                av_done[h - 16] = av_done[h]
            mask_used = S_.val("PE")
            S_.emit("vector", (lambda e: e.reciprocal(rden, den)),
                    waits=[("ACT", exp_done[15])], sem="DVE", selfwait=True)
            S_.emit("vector", (lambda e: e.tensor_tensor(
                out=OTM.rearrange("p (h d) -> p h d", h=16),
                in0=P.psA[:, 1024:2048].rearrange("p (h d) -> p h d", h=16),
                in1=rden.unsqueeze(2).broadcast_to([128, 16, 64]),
                op=ALU.mult)), waits=[("PE", av_done[15])], sem="DVE",
                selfwait=True)
            av_free = S_.val("DVE")
            otm_v = av_free
            pTo, tow = pt_take([("DVE", otm_v)])
            for kc in range(8):
                S_.emit("tensor", (lambda e, kc=kc, pTo=pTo: e.transpose(
                    pTo[:, kc * 128:(kc + 1) * 128],
                    OTM[:, kc * 128:(kc + 1) * 128], P.ident)),
                    waits=tow if kc == 0 else (), sem="PE")
            tvo = S_.val("PE")
            S_.emit("vector", (lambda e, pTo=pTo: e.tensor_copy(OFM, pTo[:])),
                    waits=[("PE", tvo)], sem="DVE")
            ofm_v = S_.val("DVE")
            pt_evac.append(ofm_v)
            yw = [("DVE", max(ofm_v, psB_free))]
            if (bl, ch - 1) in res_done:
                yw.append(("DVE", res_done[(bl, ch - 1)]))
            for kc in range(8):
                for nb in range(2):
                    S_.emit("tensor", (lambda e, kc=kc, nb=nb: e.matmul(
                        P.psB[:, nb * 512:(nb + 1) * 512],
                        OFM[:, kc * 128:(kc + 1) * 128],
                        WO[:, kc * 1024 + nb * 512:kc * 1024 + (nb + 1) * 512],
                        start=(kc == 0), stop=(kc == 7))),
                        waits=yw if (kc == 0 and nb == 0) else (), sem="PE")
            pe_y = S_.val("PE")
            rw = [("PE", pe_y), ("dX", xl)]
            if (bl, ch - 2) in out_done:
                rw.append(("dX", out_done[(bl, ch - 2)]))
            S_.emit("vector", (lambda e, ch=ch: e.tensor_add(
                XN[ch % 2], P.psB[:], XO[ch % 2])), waits=rw, sem="DVE")
            res_done[(bl, ch)] = S_.val("DVE")
            S_.emit("sync", (lambda e, ti=ti, ch=ch: e.dma_start(
                out=x_out[ti * 128:(ti + 1) * 128, :], in_=XN[ch % 2])),
                waits=[("DVE", res_done[(bl, ch)])], sem="dX", inc=16)
            out_done[(bl, ch)] = S_.val("dX")
            if _DBG and bl == 0 and ch == 4:
                S_.emit("sync", (lambda e: e.dma_start(
                    out=d["dbgf"][:, 0:16], in_=den)),
                    waits=[("ACT", exp_done[15])], sem="dX", inc=16)
                S_.emit("sync", (lambda e: e.dma_start(
                    out=d["dbgf"][:, 16:32], in_=rden)),
                    waits=[("DVE", av_free)], sem="dX", inc=16)
                S_.emit("sync", (lambda e: e.dma_start(
                    out=d["dbgb"][:, 0:1024], in_=OTM)),
                    waits=[("DVE", av_free)], sem="dX", inc=16)
                S_.emit("sync", (lambda e: e.dma_start(
                    out=d["dbgb"][:, 1024:2048], in_=OFM)),
                    waits=[("DVE", ofm_v)], sem="dX", inc=16)
                S_.emit("sync", (lambda e, ch=ch: e.dma_start(
                    out=d["dbgf"][:, 32:1056], in_=XO[ch % 2])),
                    waits=[("DVE", res_done[(bl, ch)])], sem="dX", inc=16)
                S_.emit("sync", (lambda e, ch=ch: e.dma_start(
                    out=d["dbgf"][:, 1056:2080], in_=XN[ch % 2])),
                    waits=[("DVE", res_done[(bl, ch)])], sem="dX", inc=16)
                S_.wait("vector", [("dX", S_.val("dX"))])
                S_.wait("scalar", [("dX", S_.val("dX"))])
        S_.fence()


def emit_gla(S_, P, d, x_in, x_out):
    nc = S_.nc
    smf, f32, wb, sm = P.smf, P.f32, P.wb, P.sm
    S_.fence()
    WQg = wb[:, 0:8192]
    WKg = wb[:, 8192:16384]
    WVg = wb[:, 16384:24576]
    WGg = wb[:, 24576:32768]
    WOg = wb[:, 32768:40960]
    for kc in range(8):
        for nm, dst in (("gla_wq", WQg), ("gla_wk", WKg), ("gla_wv", WVg),
                        ("gla_wg", WGg), ("gla_wo", WOg)):
            S_.emit("gpsimd", (lambda e, kc=kc, nm=nm, dst=dst: e.dma_start(
                out=dst[:, kc * 1024:(kc + 1) * 1024],
                in_=d[nm][kc * 128:(kc + 1) * 128, :])), sem="dW", inc=16)
    WGK2 = wb[:, 40960:41984]
    WGK1 = wb[:, 41984:42112]
    S_.emit("gpsimd", (lambda e: e.dma_start(out=WGK2, in_=d["gla_wgk2"])),
            sem="dW", inc=16)
    S_.emit("gpsimd", (lambda e: e.dma_start(
        out=WGK1.rearrange("p (kc c) -> p kc c", kc=8),
        in_=d["gla_wgk1"].rearrange("(kc p) c -> p kc c", p=128),
        allow_slow_non_contiguous=True)), sem="dW", inc=16)
    GNWT = f32[:, 8064:8192]
    ldg = S_.emit("gpsimd", (lambda e: e.dma_start(
        out=GNWT, in_=d["gla_gnw"])), sem="dW", inc=16)
    w_all = S_.val("dW")
    wo_folded = 0
    ONESR = sm[0:1, 128:256]
    B2HI = sm[0:1, 256:1280]
    B2LO = sm[0:1, 1280:2304]
    TRI = sm[:, 2432:2560]
    A_M = sm[:, 2304:2432]
    ldb = S_.emit("gpsimd", (lambda e: e.dma_start(
        out=f32[0:1, 5632:6656], in_=d["gla_bgk2"].unsqueeze(0))),
        sem="dW", inc=16)
    ldt = S_.emit("gpsimd", (lambda e: e.dma_start(
        out=TRI, in_=d["tri01"])), sem="dW", inc=16)
    S_.emit("vector", (lambda e: e.memset(ONESR, 1.0)), sem="DVE")

    lnw = emit_ln_transpose(S_, P, x_in)
    lnx3 = P.lnx.ap().rearrange("p (kc tok) -> p kc tok", kc=8)

    S_.emit("vector", (lambda e: e.tensor_copy(B2HI, f32[0:1, 5632:6656])),
            waits=[("dW", ldb), ("PE", lnw["PE"])], sem="DVE")
    S_.emit("vector", (lambda e: e.tensor_copy(f32[0:1, 6656:7680], B2HI)),
            sem="DVE", selfwait=True)
    S_.emit("vector", (lambda e: e.tensor_tensor(
        out=B2LO, in0=f32[0:1, 5632:6656], in1=f32[0:1, 6656:7680],
        op=ALU.subtract)), sem="DVE", selfwait=True)
    pre_v = S_.val("DVE")

    QFg = wb[:, 42112:43136]
    KFg = wb[:, 43136:44160]
    ZFg = wb[:, 44160:45184]
    VTg = wb[:, 45184:46208]
    OFMG = wb[:, 46208:47232]
    GST = wb[:, 47232:48256]
    OTMH = wb[:, 48256:48384]
    KHAT = wb[:, 48384:48512]
    QPP = wb[:, 48512:48640]
    SBF = wb[:, 48640:48768]
    KHT = wb[:, 48768:48896]
    Z1T = wb[:, 48896:49024]
    Z1BP = wb[:, 49024:49152]
    Z1B = Z1BP[:, 0:16]
    S_.emit("vector", (lambda e: e.memset(Z1BP, 0.0)), sem="DVE")
    SOLD = f32[:, 4096:5120]
    ARAW = f32[:, 5120:5248]
    SPG = f32[:, 5248:5376]
    EB = f32[:, 5376:5504]
    EINV = f32[:, 5504:5632]
    SQG = f32[:, 5632:6656]
    XO = [f32[:, 0:1024], f32[:, 1024:2048]]
    XN = [f32[:, 2048:3072], f32[:, 3072:4096]]
    ssqg = smf[:, 396:404]
    rstdg = smf[:, 408:416]
    stdg = smf[:, 416:424]
    bmid = smf[:, 392:393]
    pmidq = smf[:, 393:394]
    blastc = smf[:, 394:395]
    f2c = smf[:, 395:396]
    fdiff = smf[:, 391:392]
    c_m16, v1 = P.const_col(S_, -1.0 / GNORM)
    c_p16, v2 = P.const_col(S_, 1.0 / GNORM)
    c_qb, v3 = P.const_col(S_, float(np.log(128.0 ** -0.5)))
    c_n1, v4 = P.const_col(S_, -1.0)
    c_i128, v5 = P.const_col(S_, 1.0 / 128.0)
    c_geps, v6 = P.const_col(S_, GLA_EPS)
    c_zero, v7 = P.const_col(S_, 0.0)
    cv = max(v1, v2, v3, v4, v5, v6, v7)

    ptu = [0]
    pt_evac = []

    def pt_take(waits):
        w = list(waits)
        if ptu[0] >= 2:
            w.append(("DVE", pt_evac[ptu[0] - 2]))
        ptu[0] += 1
        return P.pT[(ptu[0] - 1) % 2], w

    psB_free = max(wo_folded, pre_v)
    opsum_free = 0
    abank_free = 0
    dbank_free = 0
    z1_free = 0
    res_done = {}
    out_done = {}
    MID = 63

    def proj(ti, W, stage_wait):
        mw = [("DVE", max(lnw["DVE"], stage_wait)), ("dW", w_all)]
        for kc in range(8):
            for nb in range(2):
                S_.emit("tensor", (lambda e, kc=kc, nb=nb, ti=ti, W=W:
                                   e.matmul(
                    P.psB[:, nb * 512:(nb + 1) * 512],
                    lnx3[:, kc, ti * 128:(ti + 1) * 128],
                    W[:, kc * 1024 + nb * 512:kc * 1024 + (nb + 1) * 512],
                    start=(kc == 0), stop=(kc == 7))),
                    waits=mw if (kc == 0 and nb == 0) else (), sem="PE")
        return S_.val("PE")

    def stage_transpose(dst3, src, nch, waits):
        pT, tw = pt_take(waits)
        for kc in range(nch):
            S_.emit("tensor", (lambda e, kc=kc, pT=pT, src=src: e.transpose(
                pT[:, kc * 128:(kc + 1) * 128],
                src[:, kc * 128:(kc + 1) * 128], P.ident)),
                waits=tw if kc == 0 else (), sem="PE")
        tv = S_.val("PE")
        S_.emit("vector", (lambda e, pT=pT, dst3=dst3, nch=nch: e.tensor_copy(
            dst3, pT[:, 0:nch * 128].rearrange("p (kc c) -> p kc c", kc=nch))),
            waits=[("PE", tv)], sem="DVE")
        pt_evac.append(S_.val("DVE"))
        return S_.val("DVE"), tv

    for bl in range(2):
        S_.emit("vector", (lambda e: e.memset(SOLD, 0.0)),
                waits=[("DVE", res_done.get((bl - 1, 15), 0))], sem="DVE",
                selfwait=True)
        sold_init = S_.val("DVE")
        supd_done = sold_init
        for ct in range(16):
            ti = bt_tile(bl, ct)
            pe_q = proj(ti, WQg, psB_free)
            S_.emit("vector", (lambda e: e.tensor_copy(GST, P.psB[:])),
                    waits=[("PE", pe_q)], sem="DVE", selfwait=True)
            qs_v = S_.val("DVE")
            qf_v, qtr_pe = stage_transpose(
                QFg.rearrange("p (kc c) -> p kc c", kc=8), GST, 8,
                [("DVE", qs_v)])
            pe_k = proj(ti, WKg, qs_v)
            S_.emit("vector", (lambda e: e.tensor_copy(GST, P.psB[:])),
                    waits=[("PE", pe_k), ("PE", qtr_pe)], sem="DVE",
                    selfwait=True)
            ks_v = S_.val("DVE")
            kf_v, ktr_pe = stage_transpose(
                KFg.rearrange("p (kc c) -> p kc c", kc=8), GST, 8,
                [("DVE", ks_v)])
            pe_v = proj(ti, WVg, ks_v)
            S_.emit("vector", (lambda e: e.tensor_copy(VTg, P.psB[:])),
                    waits=[("PE", pe_v)], sem="DVE", selfwait=True)
            vt_v = S_.val("DVE")
            zw = [("DVE", max(lnw["DVE"], z1_free)), ("dW", w_all),
                  ("DVE", dbank_free)]
            for kc in range(8):
                S_.emit("tensor", (lambda e, kc=kc, ti=ti: e.matmul(
                    P.psA[:, 1792:1808],
                    lnx3[:, kc, ti * 128:(ti + 1) * 128],
                    WGK1[:, kc * 16:(kc + 1) * 16],
                    start=(kc == 0), stop=(kc == 7))),
                    waits=zw if kc == 0 else (), sem="PE")
            pe_z1 = S_.val("PE")
            S_.emit("vector", (lambda e: e.tensor_copy(
                Z1B, P.psA[:, 1792:1808])), waits=[("PE", pe_z1)],
                sem="DVE", selfwait=True)
            z1b_v = S_.val("DVE")
            pTz, tzw = pt_take([("DVE", z1b_v)])
            S_.emit("tensor", (lambda e, pTz=pTz: e.transpose(
                pTz[:, 0:128], Z1BP, P.ident)), waits=tzw, sem="PE")
            tzv = S_.val("PE")
            S_.emit("vector", (lambda e, pTz=pTz: e.tensor_copy(
                Z1T, pTz[:, 0:128])), waits=[("PE", tzv)], sem="DVE")
            z1t_v = S_.val("DVE")
            pt_evac.append(z1t_v)
            z1_free = z1t_v
            z2w = [("DVE", max(z1t_v, vt_v)), ("PE", ktr_pe)]
            for nb in range(2):
                S_.emit("tensor", (lambda e, nb=nb: e.matmul(
                    P.psB[:, nb * 512:(nb + 1) * 512], Z1T,
                    WGK2[:, nb * 512:(nb + 1) * 512],
                    start=True, stop=False)),
                    waits=z2w if nb == 0 else (), sem="PE")
                S_.emit("tensor", (lambda e, nb=nb: e.matmul(
                    P.psB[:, nb * 512:(nb + 1) * 512], ONESR,
                    B2HI[:, nb * 512:(nb + 1) * 512],
                    start=False, stop=False)), sem="PE")
                S_.emit("tensor", (lambda e, nb=nb: e.matmul(
                    P.psB[:, nb * 512:(nb + 1) * 512], ONESR,
                    B2LO[:, nb * 512:(nb + 1) * 512],
                    start=False, stop=True)), sem="PE")
            pe_z2 = S_.val("PE")
            S_.emit("vector", (lambda e: e.tensor_copy(GST, P.psB[:])),
                    waits=[("PE", pe_z2)], sem="DVE", selfwait=True)
            zs_v = S_.val("DVE")
            if _DBG and bl == 0 and ct == 0:
                S_.emit("gpsimd", (lambda e: e.dma_start(
                    out=d["dbgf"][:, 3368:3384], in_=Z1B)),
                    waits=[("DVE", z1b_v)], sem="dW", inc=16)
                S_.emit("gpsimd", (lambda e: e.dma_start(
                    out=d["dbgf"][0:16, 3384:3512], in_=Z1T[0:16, :])),
                    waits=[("DVE", z1t_v)], sem="dW", inc=16)
                S_.emit("sync", (lambda e: e.dma_start(
                    out=d["dbgb"][:, 3584:4096], in_=GST[:, 0:512])),
                    waits=[("DVE", zs_v)], sem="dX", inc=16)
                S_.wait("vector", [("dX", S_.val("dX"))])
                S_.wait("vector", [("dW", S_.val("dW"))])
            zf_v, ztr_pe = stage_transpose(
                ZFg.rearrange("p (kc c) -> p kc c", kc=8), GST, 8,
                [("DVE", zs_v)])
            pe_g = proj(ti, WGg, zs_v)
            S_.emit("scalar", (lambda e: e.activation(
                out=GST, in_=P.psB[:], func=AF.Silu)),
                waits=[("PE", pe_g), ("PE", ztr_pe)], sem="ACT",
                selfwait=True)
            gs_v = S_.val("ACT")
            psB_free_after_g = gs_v

            xw = []
            if (bl, ct - 2) in res_done:
                xw.append(("DVE", res_done[(bl, ct - 2)]))
            if bl == 0 and ct < 2:
                xw += [("ACT", lnw["ACT"]), ("DVE", lnw["DVE"])]
            xl = S_.emit("sync", (lambda e, ti=ti, ct=ct: e.dma_start(
                out=XO[ct % 2], in_=x_in[ti * 128:(ti + 1) * 128, :])),
                waits=xw, sem="dX", inc=16)

            onorm_prev = opsum_free
            qks_prev = 0
            ds_pe_prev = 0
            for h in range(8):
                zf_h = ZFg[:, h * 128:(h + 1) * 128]
                qf_h = QFg[:, h * 128:(h + 1) * 128]
                kf_h = KFg[:, h * 128:(h + 1) * 128]
                vt_h = VTg[:, h * 128:(h + 1) * 128]
                so_h = SOLD[:, h * 128:(h + 1) * 128]
                S_.emit("scalar", (lambda e, zf_h=zf_h: e.activation(
                    out=EB, in_=zf_h, func=AF.Sigmoid, bias=c_zero)),
                    waits=[("DVE", max(zf_v, cv, supd_done))], sem="ACT",
                    selfwait=True)
                S_.emit("scalar", (lambda e: e.activation(
                    out=SPG, in_=EB, func=AF.Ln)), sem="ACT", selfwait=True)
                sp_v = S_.val("ACT")
                S_.emit("vector", (lambda e: e.tensor_tensor_scan(
                    out=ARAW, data0=SPG, data1=SPG, initial=0.0,
                    op0=ALU.add, op1=ALU.bypass)),
                    waits=[("ACT", sp_v)], sem="DVE", selfwait=True)
                araw_v = S_.val("DVE")
                S_.emit("scalar", (lambda e: e.activation(
                    out=bmid, in_=ARAW[:, MID:MID + 1], func=AF.Exp,
                    scale=c_p16)), waits=[("DVE", araw_v)], sem="ACT",
                    selfwait=True)
                S_.emit("scalar", (lambda e: e.activation(
                    out=pmidq, in_=ARAW[:, MID:MID + 1], func=AF.Exp,
                    scale=c_m16, bias=c_qb)), sem="ACT", selfwait=True)
                S_.emit("scalar", (lambda e: e.activation(
                    out=blastc, in_=ARAW[:, 127:128], func=AF.Exp,
                    scale=c_p16)), sem="ACT", selfwait=True)
                S_.emit("vector", (lambda e: e.tensor_tensor(
                    out=fdiff, in0=ARAW[:, 127:128],
                    in1=ARAW[:, MID:MID + 1], op=ALU.subtract)),
                    sem="DVE", selfwait=True)
                fd_v = S_.val("DVE")
                S_.emit("scalar", (lambda e: e.activation(
                    out=f2c, in_=fdiff, func=AF.Exp, scale=c_p16)),
                    waits=[("DVE", fd_v)], sem="ACT", selfwait=True)
                S_.emit("scalar", (lambda e: e.activation(
                    out=EB, in_=ARAW, func=AF.Exp, scale=c_p16)),
                    sem="ACT", selfwait=True)
                S_.emit("scalar", (lambda e: e.activation(
                    out=EINV, in_=ARAW, func=AF.Exp, scale=c_m16)),
                    sem="ACT", selfwait=True)
                exps_v = S_.val("ACT")
                S_.emit("vector", (lambda e, qf_h=qf_h:
                                   e.scalar_tensor_tensor(
                    out=QPP, in0=EB, scalar=pmidq, in1=qf_h,
                    op0=ALU.mult, op1=ALU.mult)),
                    waits=[("ACT", exps_v), ("PE", ds_pe_prev)], sem="DVE",
                    selfwait=True)
                S_.emit("vector", (lambda e, kf_h=kf_h:
                                   e.scalar_tensor_tensor(
                    out=KHAT, in0=EINV, scalar=bmid, in1=kf_h,
                    op0=ALU.mult, op1=ALU.mult)), sem="DVE", selfwait=True)
                S_.emit("vector", (lambda e, so_h=so_h: e.tensor_scalar(
                    out=SBF, in0=so_h, scalar1=bmid, scalar2=None,
                    op0=ALU.mult)),
                    waits=[("DVE", supd_done)], sem="DVE", selfwait=True)
                qks_v = S_.val("DVE")
                aw = [("DVE", max(qks_v, abank_free))]
                S_.emit("tensor", (lambda e: e.matmul(
                    P.psA[:, 1024:1152], KHAT, QPP, start=True, stop=True)),
                    waits=aw, sem="PE")
                a_pe = S_.val("PE")
                S_.emit("vector", (lambda e: e.tensor_tensor(
                    out=A_M, in0=P.psA[:, 1024:1152], in1=TRI,
                    op=ALU.mult)),
                    waits=[("PE", a_pe), ("dW", ldt)], sem="DVE",
                    selfwait=True)
                am_v = S_.val("DVE")
                abank_free = am_v
                pTh, thw = pt_take([("DVE", qks_v)])
                S_.emit("tensor", (lambda e, pTh=pTh: e.transpose(
                    pTh[:, 0:128], KHAT, P.ident)), waits=thw, sem="PE")
                th_pe = S_.val("PE")
                S_.emit("vector", (lambda e, pTh=pTh: e.tensor_copy(
                    KHT, pTh[:, 0:128])), waits=[("PE", th_pe)], sem="DVE")
                kht_v = S_.val("DVE")
                pt_evac.append(kht_v)
                ow = [("DVE", max(am_v, vt_v)), ("DVE", onorm_prev)]
                S_.emit("tensor", (lambda e, vt_h=vt_h, h=h: e.matmul(
                    P.psA[:, h * 128:(h + 1) * 128], A_M, vt_h,
                    start=True, stop=False)), waits=ow, sem="PE")
                S_.emit("tensor", (lambda e, h=h: e.matmul(
                    P.psA[:, h * 128:(h + 1) * 128], QPP, SBF,
                    start=False, stop=True)), sem="PE")
                o_pe = S_.val("PE")
                dw_ = [("DVE", max(kht_v, dbank_free))]
                S_.emit("tensor", (lambda e, vt_h=vt_h: e.matmul(
                    P.psA[:, 1536:1664], KHT, vt_h, start=True, stop=True)),
                    waits=dw_, sem="PE")
                ds_pe = S_.val("PE")
                S_.emit("vector", (lambda e, so_h=so_h: e.tensor_scalar(
                    out=so_h, in0=so_h, scalar1=blastc, scalar2=None,
                    op0=ALU.mult)),
                    waits=[("PE", ds_pe)], sem="DVE", selfwait=True)
                S_.emit("vector", (lambda e, so_h=so_h:
                                   e.scalar_tensor_tensor(
                    out=so_h, in0=P.psA[:, 1536:1664], scalar=f2c,
                    in1=so_h, op0=ALU.mult, op1=ALU.add)),
                    sem="DVE", selfwait=True)
                supd_done = S_.val("DVE")
                dbank_free = supd_done
                qks_prev = qks_v
                ds_pe_prev = ds_pe
                if _DBG and bl == 0 and ct == 0 and h == 0:
                    S_.emit("sync", (lambda e: e.dma_start(
                        out=d["dbgf"][:, 2080:2208], in_=ARAW)),
                        waits=[("DVE", araw_v)], sem="dX", inc=16)
                    S_.emit("sync", (lambda e: e.dma_start(
                        out=d["dbgb"][:, 2176:2304], in_=QPP)),
                        waits=[("DVE", qks_v)], sem="dX", inc=16)
                    S_.emit("sync", (lambda e: e.dma_start(
                        out=d["dbgb"][:, 2304:2432], in_=KHAT)),
                        waits=[("DVE", qks_v)], sem="dX", inc=16)
                    S_.emit("sync", (lambda e: e.dma_start(
                        out=d["dbgb"][:, 2048:2176], in_=A_M)),
                        waits=[("DVE", am_v)], sem="dX", inc=16)
                    S_.emit("sync", (lambda e: e.dma_start(
                        out=d["dbgf"][:, 3240:3368],
                        in_=SOLD[:, 0:128])),
                        waits=[("DVE", supd_done)], sem="dX", inc=16)
                    S_.wait("vector", [("dX", S_.val("dX"))])
                    S_.wait("scalar", [("dX", S_.val("dX"))])
            opsum_pe = S_.val("PE")
            S_.emit("scalar", (lambda e: e.activation(
                out=SQG, in_=P.psA[:, 0:1024], func=AF.Square)),
                waits=[("PE", opsum_pe)], sem="ACT", selfwait=True)
            sqg_v = S_.val("ACT")
            S_.emit("vector", (lambda e: e.reduce_sum(
                ssqg, SQG.rearrange("p (h d) -> p h d", h=8), axis=AXX)),
                waits=[("ACT", sqg_v)], sem="DVE", selfwait=True)
            ssqg_v = S_.val("DVE")
            S_.emit("scalar", (lambda e: e.activation(
                out=stdg, in_=ssqg, func=AF.Sqrt, scale=c_i128,
                bias=c_geps)), waits=[("DVE", ssqg_v)], sem="ACT",
                selfwait=True)
            rg_v = S_.val("ACT")
            S_.emit("vector", (lambda e: e.reciprocal(rstdg, stdg)),
                    waits=[("ACT", rg_v)], sem="DVE", selfwait=True)
            ofm_parts = []
            OGN = f32[:, 7936:8064]
            for h in range(8):
                hw = [("ACT", max(rg_v, gs_v)), ("dW", ldg)]
                if h >= 1:
                    hw.append(("PE", ofm_parts[h - 1][1]))
                S_.emit("vector", (lambda e, h=h: e.tensor_tensor(
                    out=OGN, in0=P.psA[:, h * 128:(h + 1) * 128],
                    in1=GNWT, op=ALU.mult)), waits=hw, sem="DVE",
                    selfwait=True)
                S_.emit("vector", (lambda e, h=h: e.scalar_tensor_tensor(
                    out=OTMH, in0=OGN,
                    scalar=rstdg[:, h:h + 1],
                    in1=GST[:, h * 128:(h + 1) * 128],
                    op0=ALU.mult, op1=ALU.mult)), sem="DVE",
                    selfwait=True)
                oh_v = S_.val("DVE")
                pTo, tow = pt_take([("DVE", oh_v)])
                S_.emit("tensor", (lambda e, pTo=pTo: e.transpose(
                    pTo[:, 0:128], OTMH, P.ident)), waits=tow, sem="PE")
                to_pe = S_.val("PE")
                S_.emit("vector", (lambda e, pTo=pTo, h=h: e.tensor_copy(
                    OFMG[:, h * 128:(h + 1) * 128], pTo[:, 0:128])),
                    waits=[("PE", to_pe)], sem="DVE")
                ev = S_.val("DVE")
                pt_evac.append(ev)
                ofm_parts.append((ev, to_pe))
            ofm_v = S_.val("DVE")
            opsum_free = ofm_v
            yw = [("DVE", ofm_v), ("ACT", psB_free_after_g)]
            if (bl, ct - 1) in res_done:
                yw.append(("DVE", res_done[(bl, ct - 1)]))
            for kc in range(8):
                for nb in range(2):
                    S_.emit("tensor", (lambda e, kc=kc, nb=nb: e.matmul(
                        P.psB[:, nb * 512:(nb + 1) * 512],
                        OFMG[:, kc * 128:(kc + 1) * 128],
                        WOg[:, kc * 1024 + nb * 512:
                            kc * 1024 + (nb + 1) * 512],
                        start=(kc == 0), stop=(kc == 7))),
                        waits=yw if (kc == 0 and nb == 0) else (), sem="PE")
            pe_y = S_.val("PE")
            rw = [("PE", pe_y), ("dX", xl)]
            if (bl, ct - 2) in out_done:
                rw.append(("dX", out_done[(bl, ct - 2)]))
            S_.emit("vector", (lambda e, ct=ct: e.tensor_add(
                XN[ct % 2], P.psB[:], XO[ct % 2])), waits=rw, sem="DVE")
            res_done[(bl, ct)] = S_.val("DVE")
            psB_free = res_done[(bl, ct)]
            S_.emit("sync", (lambda e, ti=ti, ct=ct: e.dma_start(
                out=x_out[ti * 128:(ti + 1) * 128, :], in_=XN[ct % 2])),
                waits=[("DVE", res_done[(bl, ct)])], sem="dX", inc=16)
            out_done[(bl, ct)] = S_.val("dX")
            if _DBG and bl == 0 and ct == 0:
                S_.emit("sync", (lambda e, ct=ct: e.dma_start(
                    out=d["dbgf"][:, 0:1024], in_=XO[ct % 2])),
                    waits=[("DVE", res_done[(bl, ct)])], sem="dX", inc=16)
                S_.emit("gpsimd", (lambda e: e.dma_start(
                    out=d["dbgf"][:, 1024:1152], in_=WOg[:, 0:128])),
                    waits=[("DVE", wo_folded)], sem="dW", inc=16)
                S_.emit("sync", (lambda e: e.dma_start(
                    out=d["dbgb"][:, 0:1024], in_=GST)),
                    waits=[("ACT", gs_v)], sem="dX", inc=16)
                S_.emit("sync", (lambda e: e.dma_start(
                    out=d["dbgb"][:, 1024:2048], in_=OFMG)),
                    waits=[("DVE", ofm_v)], sem="dX", inc=16)
                S_.emit("sync", (lambda e: e.dma_start(
                    out=d["dbgf"][:, 2208:2216], in_=rstdg)),
                    waits=[("DVE", ofm_v)], sem="dX", inc=16)
                S_.emit("sync", (lambda e, ct=ct: e.dma_start(
                    out=d["dbgf"][:, 2216:3240], in_=XN[ct % 2])),
                    waits=[("DVE", res_done[(bl, ct)])], sem="dX", inc=16)
                S_.emit("sync", (lambda e: e.dma_start(
                    out=d["dbgb"][:, 2432:3456], in_=VTg)),
                    waits=[("DVE", vt_v)], sem="dX", inc=16)
                S_.emit("sync", (lambda e: e.dma_start(
                    out=d["dbgb"][:, 3456:3584], in_=ZFg[:, 0:128])),
                    waits=[("DVE", zf_v)], sem="dX", inc=16)
                S_.wait("vector", [("dX", S_.val("dX"))])
                S_.wait("scalar", [("dX", S_.val("dX"))])
    S_.fence()


FULL_PLAN = ["ffn0", "A2A", "attn", "A2A", "ffn0", "A2A",
             "bitq", "bitffn", "gla", "bitffn"]
_PLAN = None
_DBG = False

COMPUTE = {"ffn0", "attn", "bitffn", "gla"}


def emit_a2a(S_, src, dst):
    S_.fence()
    S_.emit("gpsimd", (lambda e: e.collective_compute(
        "AllToAll", ALU.bypass, RG8, ins=[src], outs=[dst])), sem="GP")
    S_.fence()


def build_program(plan=None):
    plan = (FULL_PLAN if _PLAN is None else _PLAN) if plan is None else plan
    nc = bass.Bass("TRN2", target_bir_lowering=False)
    d = {}
    wnames = [
        ("x0", [NTOK, D], F32),
        ("ffn0_wx_t", [D, 3072], F32), ("ffn0_ux_t", [D, 3072], F32),
        ("ffn0_wo_t", [D, D], F32), ("ffn0_uo_t", [D, D], F32),
        ("ffn0_b_cat", [3072], F32), ("ffn0_bo", [D], F32),
        ("ffn0_in_c", [D], F32),
        ("attn_wq", [D, D], F32), ("attn_wk", [D, 256], F32),
        ("attn_wv", [D, 256], F32), ("attn_wo", [D, D], F32),
        ("gla_wq", [D, D], F32), ("gla_wk", [D, D], F32),
        ("gla_wv", [D, D], F32), ("gla_wg", [D, D], F32),
        ("gla_wgk1", [D, GLR], F32), ("gla_wgk2", [128, D], F32),
        ("gla_bgk2", [D], F32), ("gla_gnw", [128, 128], F32),
        ("gla_wo", [D, D], F32),
        ("bit_w1s", [128, 4096], F32), ("bit_w2s", [512, 1024], F32),
        ("bit_b1", [4096], F32), ("bit_b2", [D], F32),
        ("rope_cs", [S, 64], F32),
    ]
    for nm, shp, dt in wnames:
        d[nm] = nc.dram_tensor(nm, shp, dt, kind="ExternalInput").ap()
    d["attn_mask"] = nc.dram_tensor("attn_mask", [S, 448], BF16,
                                    kind="ExternalInput").ap()
    d["tri01"] = nc.dram_tensor("tri01", [128, 128], BF16,
                                kind="ExternalInput").ap()

    out = nc.dram_tensor("out", [NTOK, D], F32, kind="ExternalOutput").ap()
    if _DBG:
        d["dbgf"] = nc.dram_tensor("dbgf", [128, 4096], F32,
                                   kind="ExternalOutput").ap()
        d["dbgb"] = nc.dram_tensor("dbgb", [128, 4096], BF16,
                                   kind="ExternalOutput").ap()
        d["dbgb2"] = nc.dram_tensor("dbgb2", [128, 4096], BF16,
                                    kind="ExternalOutput").ap()
    d["xq2_d"] = nc.dram_tensor("xq2_d", [NTOK, 4096], BF16,
                                kind="Internal").ap()
    d["w1q_d"] = nc.dram_tensor("w1q_d", [D, 4096], BF16, kind="Internal").ap()
    d["w2q_d"] = nc.dram_tensor("w2q_d", [4096, D], BF16, kind="Internal").ap()
    d["w1q_in"] = nc.dram_tensor("w1q_in", [1024, 4096], BF16,
                                 kind="Internal").ap()
    d["w2q_in"] = nc.dram_tensor("w2q_in", [4096, 1024], BF16,
                                 kind="Internal").ap()
    d["cc_in"] = nc.dram_tensor("cc_in", [4096, 1024], F32,
                                kind="Internal").ap()
    d["cc_out"] = nc.dram_tensor("cc_out", [4096, 1024], F32,
                                 kind="Internal").ap()

    n_bufs = sum(1 for s in plan if s in COMPUTE or s == "A2A") - 1
    xs = [nc.dram_tensor(f"xs{i}", [NTOK, D], F32, kind="Internal").ap()
          for i in range(n_bufs)]

    S_ = Sched(nc)
    P = Pools(nc)
    ident_dram = nc.inline_tensor(np.eye(128, dtype=BF), "ident_c")
    S_.emit("sync", (lambda e: e.dma_start(out=P.ident, in_=ident_dram.ap())),
            sem="dX", inc=16)
    for eng in ("scalar", "tensor", "vector", "gpsimd"):
        S_.wait(eng, [("dX", S_.val("dX"))])

    cur = d["x0"]
    bi = 0
    n_flow = sum(1 for s in plan if s in COMPUTE or s == "A2A")
    fi = 0
    for st in plan:
        if st == "bitq":
            emit_bit_wquant(S_, P, d)
            continue
        fi += 1
        dst = out if fi == n_flow else xs[bi]
        bi += 1
        if st == "A2A":
            emit_a2a(S_, cur, dst)
        elif st == "ffn0":
            emit_ffn0(S_, P, d, cur, dst)
        elif st == "attn":
            emit_attn(S_, P, d, cur, dst)
        elif st == "bitffn":
            emit_bitffn(S_, P, d, cur, dst)
        elif st == "gla":
            emit_gla(S_, P, d, cur, dst)
        cur = dst

    S_.wait("sync", [("dX", S_.val("dX"))])
    S_.replay()
    return nc


def _prep_core_inputs(inputs, core, batch_x0=False):
    if batch_x0:
        xb = np.empty((8, 2, 256, D), dtype=np.float32)
        for R in range(8):
            for bl in range(2):
                xb[R, bl] = inputs["L0_attn_x"][2 * core + bl,
                                                R * 256:(R + 1) * 256, :]
        x = xb.reshape(NTOK, D)
    else:
        x = np.ascontiguousarray(
            inputs["L0_attn_x"][:, core * SL:(core + 1) * SL, :]
        ).reshape(NTOK, D)
    m = {"x0": x}
    wf, wi, wc = inputs["L0_ffn_wf"], inputs["L0_ffn_wi"], inputs["L0_ffn_wc"]
    uf, ui, uc = inputs["L0_ffn_uf"], inputs["L0_ffn_ui"], inputs["L0_ffn_uc"]
    m["ffn0_wx_t"] = np.ascontiguousarray(
        np.concatenate([wf.T, wi.T, wc.T], axis=1))
    m["ffn0_ux_t"] = np.ascontiguousarray(
        np.concatenate([uf.T, ui.T, uc.T], axis=1))
    m["ffn0_wo_t"] = np.ascontiguousarray(inputs["L0_ffn_wo"].T)
    m["ffn0_uo_t"] = np.ascontiguousarray(inputs["L0_ffn_uo"].T)
    m["ffn0_b_cat"] = np.ascontiguousarray(np.concatenate(
        [inputs["L0_ffn_bf"], inputs["L0_ffn_bi"], inputs["L0_ffn_bc"]]))
    m["ffn0_bo"] = np.ascontiguousarray(inputs["L0_ffn_bo"])
    m["ffn0_in_c"] = np.ascontiguousarray(inputs["L0_ffn_in_c"])
    m["attn_wq"] = np.ascontiguousarray(
        inputs["L0_attn_Wq"].reshape(D, AH, HD)[:, QPERM, :].reshape(D, D))
    for k, src in (("attn_wk", "L0_attn_Wk"),
                   ("attn_wv", "L0_attn_Wv"), ("attn_wo", "L0_attn_Wo"),
                   ("gla_wq", "L1_attn_Wq"), ("gla_wk", "L1_attn_Wk"),
                   ("gla_wv", "L1_attn_Wv"), ("gla_wg", "L1_attn_Wg"),
                   ("gla_wgk1", "L1_attn_Wgk1"),
                   ("gla_bgk2", "L1_attn_bgk2"),
                   ("gla_wo", "L1_attn_Wo"),
                   ("bit_b1", "L1_ffn_b1"), ("bit_b2", "L1_ffn_b2")):
        m[k] = np.ascontiguousarray(inputs[src])
    w1t = inputs["L1_ffn_W1"].T
    w2t = inputs["L1_ffn_W2"].T
    m["bit_w1s"] = np.ascontiguousarray(w1t[core * 128:(core + 1) * 128, :])
    m["bit_w2s"] = np.ascontiguousarray(w2t[core * 512:(core + 1) * 512, :])
    wgk2p = np.zeros((128, D), np.float32)
    wgk2p[0:GLR] = inputs["L1_attn_Wgk2"]
    m["gla_wgk2"] = wgk2p
    m["gla_gnw"] = np.ascontiguousarray(np.tile(
        np.asarray(inputs["L1_attn_g_norm_w"],
                   np.float32).reshape(1, 128), (128, 1)))
    m["rope_cs"] = _ROPE_CS
    m["attn_mask"] = _ATTN_MASK
    m["tri01"] = _TRI01
    return m


def _mk_rope():
    inv = 1.0 / (ROPE_BASE ** (np.arange(0, HD, 2, dtype=np.float32) / HD))
    fr = np.arange(S, dtype=np.float32)[:, None] * inv[None, :]
    return np.concatenate([np.cos(fr), np.sin(fr)], axis=1).astype(np.float32)


def _mk_mask():
    mask = np.full((S, 448), NEGBIG, np.float32)
    qi = np.arange(S)
    for ch in range(16):
        rows = slice(ch * 128, (ch + 1) * 128)
        q = qi[rows][:, None]
        if ch >= 3:
            kg = np.arange(GT)[None, :]
            kw = ((ch - 2) * 128 + np.arange(384))[None, :]
            ag = kg <= q
            aw = (kw <= q) & ((kw >= q - LW + 1) | (kw < GT))
            allowed = np.concatenate([ag, aw], axis=1)
            mask[rows, :] = np.where(allowed, 0.0, NEGBIG)
        else:
            n = (ch + 1) * 128
            k = np.arange(n)[None, :]
            allowed = (k <= q) & ((k >= q - LW + 1) | (k < GT))
            mask[rows, 0:n] = np.where(allowed, 0.0, NEGBIG)
    return mask.astype(BF)


_ROPE_CS = _mk_rope()
_ATTN_MASK = _mk_mask()
_TRI01 = np.triu(np.ones((128, 128), np.float32)).astype(BF)


def _gather_batch(res):
    full = np.empty((B, S, D), dtype=np.float32)
    for c in range(NCORES):
        o = res.results[c]["out"].reshape(8, 2, 256, D)
        for bl in range(2):
            full[2 * c + bl] = o[:, bl].reshape(S, D)
    return full


def _gather_seq(res):
    full = np.empty((B, S, D), dtype=np.float32)
    for c in range(NCORES):
        full[:, c * SL:(c + 1) * SL, :] = (
            res.results[c]["out"].reshape(B, SL, D))
    return full


def kernel(**inputs):
    plan = FULL_PLAN if _PLAN is None else _PLAN
    batch_in = plan[0] not in ("ffn0", "A2A")
    n_a2a = sum(1 for s in plan if s == "A2A")
    batch_out = (n_a2a % 2 == 1) or batch_in
    nc = build_program(plan)
    in_maps = [_prep_core_inputs(inputs, c, batch_x0=batch_in)
               for c in range(NCORES)]
    res = run_bass_kernel_spmd(nc, in_maps, core_ids=list(range(NCORES)))
    global _LAST
    _LAST = res
    return _gather_batch(res) if batch_out else _gather_seq(res)
